# version 04
# baseline (speedup 1.0000x reference)
import numpy as np
from contextlib import ExitStack

import concourse.bass as bass
import concourse.mybir as mybir
from concourse.bass_utils import run_bass_kernel_spmd
from concourse.alu_op_type import AluOpType as OP

F32 = mybir.dt.float32
BF16 = mybir.dt.bfloat16
I32 = mybir.dt.int32
AF = mybir.ActivationFunctionType

NCORE = 8
S = 2048
D = 4096
H = 32
HK = 8
HD = 128
GU_M = 172
DN_K = 86
SCALE = 1.0 / float(np.sqrt(HD))
SAFETY = 1.05

ENGINES = ("tensor", "vector", "scalar", "sync")


class _Stop(Exception):
    pass


class Prog:

    NLANE = 8

    def __init__(self, nc, es):
        self.nc = nc
        self.blk = es.enter_context(nc.Block())
        self.comp_sem = {e: es.enter_context(nc.semaphore(f"c_{e}")) for e in ENGINES}
        self.dma_sem = {e: [es.enter_context(nc.semaphore(f"d_{e}{i}"))
                            for i in range(self.NLANE)]
                        for e in ("sync", "scalar", "vector", "tensor")}
        self.dma_count = {e: 0 for e in self.dma_sem}
        self.ops = []
        self.n_ops = {e: 0 for e in ENGINES}
        self.pending_base = {e: 0 for e in ENGINES}
        self.awaited = set()
        self.resolved = {}
        self.comp_emitted = {e: 0 for e in ENGINES}
        self.last_tok = {e: None for e in ENGINES}

    def do(self, eng, fn, wait=(), dma=False):
        tok = (eng, self.pending_base[eng] + self.n_ops[eng], bool(dma))
        self.n_ops[eng] += 1
        waits = [w for w in wait if w is not None]
        for w in waits:
            if w not in self.resolved:
                self.awaited.add(w)
        self.ops.append((eng, fn, waits, bool(dma), tok))
        self.last_tok[eng] = tok
        return tok

    def flush(self):
        if not self.ops:
            return
        for e in ENGINES:
            t = self.last_tok[e]
            if t is not None and t not in self.resolved:
                self.awaited.add(t)
        per_eng = {e: [] for e in ENGINES}
        for (eng, fn, waits, is_dma, tok) in self.ops:
            per_eng[eng].append((fn, waits, is_dma, tok))
        for e in ENGINES:
            ndma = self.dma_count[e]
            ncomp = self.comp_emitted[e]
            for (fn, waits, is_dma, tok) in per_eng[e]:
                if is_dma:
                    lane = ndma % self.NLANE
                    self.resolved[tok] = (self.dma_sem[e][lane],
                                          16 * (ndma // self.NLANE + 1))
                    ndma += 1
                elif tok in self.awaited:
                    ncomp += 1
                    self.resolved[tok] = (self.comp_sem[e], ncomp)
        for e in ENGINES:
            if not per_eng[e]:
                continue
            seq = per_eng[e]

            def section(engobj, seq=seq, e=e):
                for (fn, waits, is_dma, tok) in seq:
                    tgt = {}
                    for w in waits:
                        sem, val = self.resolved[w]
                        prev = tgt.get(id(sem))
                        tgt[id(sem)] = (sem, val if prev is None else max(val, prev[1]))
                    for sem, val in tgt.values():
                        engobj.wait_ge(sem, val)
                    if is_dma:
                        nd = self.dma_count[e]
                        lane = nd % self.NLANE
                        if nd >= self.NLANE:
                            engobj.wait_ge(self.dma_sem[e][lane],
                                           16 * (nd // self.NLANE))
                        inst = fn(engobj)
                        self.dma_count[e] += 1
                        inst.then_inc(self.dma_sem[e][lane], 16)
                        continue
                    inst = fn(engobj)
                    if False:
                        pass
                    elif tok in self.awaited:
                        self.comp_emitted[e] += 1
                        inst.then_inc(self.comp_sem[e], 1)

            getattr(self.blk, e)(section)
        for e in ENGINES:
            self.pending_base[e] += self.n_ops[e]
            self.n_ops[e] = 0
        self.ops = []

    def drain_dmas(self):
        self.flush()
        for e in ("sync", "scalar"):
            cnt = self.dma_count[e]
            if cnt == 0:
                continue
            sems = self.dma_sem[e]

            def section(engobj, cnt=cnt, sems=sems):
                for lane in range(self.NLANE):
                    n = (cnt - lane + self.NLANE - 1) // self.NLANE
                    if n > 0:
                        engobj.wait_ge(sems[lane], 16 * n)

            getattr(self.blk, e)(section)

    def join(self):
        toks = [t for t in self.last_tok.values() if t is not None]
        newt = {}
        for e in ENGINES:
            newt[e] = self.do(e, lambda x: x.nop(), wait=toks)
        self.flush()
        return newt


def build_program(debug=(), upto='p6', races=True):
    nc = bass.Bass("TRN2", target_bir_lowering=False,
                   detect_race_conditions=races)
    es = ExitStack()
    try:
        _build_body(nc, es, debug, upto)
    except _Stop:
        pass
    es.close()
    return nc


def _build_body(nc, es, debug, upto):

    def dram(name, shape, dt):
        if name in debug:
            return nc.dram_tensor(name, shape, dt, kind="ExternalOutput")
        return nc.dram_tensor(name, shape, dt)

    d_xT = nc.dram_tensor("xT", [D, S], F32, kind="ExternalInput")
    d_wq = nc.dram_tensor("wq_p", [H, 128, 4096], F32, kind="ExternalInput")
    d_wo = nc.dram_tensor("wo_p", [32, 128, 4096], F32, kind="ExternalInput")
    d_wkvd = nc.dram_tensor("wkvd", [D, 512], F32, kind="ExternalInput")
    d_wkup = nc.dram_tensor("wkup4", [4, 128, 1024], F32, kind="ExternalInput")
    d_wvup = nc.dram_tensor("wvup4", [4, 128, 1024], F32, kind="ExternalInput")
    d_qwgu = nc.dram_tensor("qwgu_p", [GU_M, 128, 512], I32, kind="ExternalInput")
    d_qzgu = nc.dram_tensor("qzgu", [32, 2752], I32, kind="ExternalInput")
    d_sgu = nc.dram_tensor("sgu", [32, 22016], F32, kind="ExternalInput")
    d_qwdn = nc.dram_tensor("qwdn_p", [32, 128, 1376], I32, kind="ExternalInput")
    d_qzdn = nc.dram_tensor("qzdn", [86, 512], I32, kind="ExternalInput")
    d_sdn = nc.dram_tensor("sdn", [86, 4096], F32, kind="ExternalInput")
    d_indgu = nc.dram_tensor("indgu", [128, 32 * 32], F32, kind="ExternalInput")
    d_inddn = nc.dram_tensor("inddn", [128, 86 * 86], F32, kind="ExternalInput")
    d_mtri = nc.dram_tensor("mtri", [128, 128], F32, kind="ExternalInput")

    d_xnT = dram("xnT_d", [32, 128, S], BF16)
    d_qT = dram("qT_d", [H, 128, S], BF16)
    d_kT = dram("kT_d", [HK, 128, S], BF16)
    d_v = dram("v_d", [16, 128, HK * HD], BF16)
    d_negc = dram("negc_d", [H, S], BF16)
    d_sgur = dram("sgur_d", [GU_M, 4096], BF16)
    d_zsgu = dram("zsgu_d", [32, 22016], BF16)
    d_sdnr = dram("sdnr_d", [32, 11008], BF16)
    d_zsdn = dram("zsdn_d", [86, 4096], BF16)
    d_hT = dram("hT_d", [DN_K, 128, S], BF16)
    d_w1dn = dram("w1dn_d", [32, 128, 11008], BF16)
    d_gT = dram("gT_d", [32, 128, S], BF16)
    d_oT = dram("oT_d", [H, 128, S], BF16)
    d_out = nc.dram_tensor("outT", [D, S], F32, kind="ExternalOutput")

    sbuf = lambda name, shape, dt: es.enter_context(nc.sbuf_tensor(name, shape, dt))
    P = Prog(nc, es)

    ones1 = sbuf("ones1", [1, 128], BF16)
    ones1f = sbuf("ones1f", [1, 128], F32)
    ones128 = sbuf("ones128", [128, 1], BF16)
    sqkn = sbuf("sqkn", [1, 8], F32)
    xsumn = sbuf("xsumn", [32, S], BF16)
    row1 = sbuf("row1", [1, 512], F32)
    row2 = sbuf("row2", [1, 512], F32)
    c0row = sbuf("c0row", [1, 32], F32)

    P.do("vector", lambda v: v.memset(ones1[:], 1.0))
    P.do("vector", lambda v: v.memset(ones1f[:], 1.0))
    P.do("vector", lambda v: v.memset(ones128[:], 1.0))

    with ExitStack() as ph:
        pb = lambda n, sh, dt: ph.enter_context(nc.sbuf_tensor(n, sh, dt))
        sf = pb("p0_sf", [86, 5504], F32)
        sbf = pb("p0_sb", [86, 5504], BF16)
        qzi = pb("p0_qz", [86, 688], I32)
        zi = pb("p0_zi", [86, 5504], I32)
        zbf = pb("p0_zb", [86, 5504], BF16)
        zsb = pb("p0_zs", [86, 5504], BF16)

        vtok = None
        for ch in range(4):
            io0 = ch * 5504
            w1 = P.do("sync", lambda sy, io0=io0: sy.dma_start(
                sf[0:32, :], d_sgu[:, io0:io0 + 5504]), wait=(vtok,), dma=True)
            w2 = P.do("sync", lambda sy, io0=io0: sy.dma_start(
                qzi[0:32, :], d_qzgu[:, io0 // 8: io0 // 8 + 688]), dma=True)
            P.do("vector", lambda v: v.tensor_copy(sbf[0:32, :], sf[0:32, :]),
                 wait=(w1, vtok))
            for idx in range(8):
                P.do("vector", lambda v, idx=idx: v.tensor_scalar(
                    bass.AP(zi, idx, [[5504, 32], [8, 688]]), qzi[0:32, :],
                    4 * idx, 15, OP.logical_shift_right, OP.bitwise_and), wait=(w2,))
            P.do("vector", lambda v: v.tensor_copy(zbf[0:32, :], zi[0:32, :]))
            vt = P.do("vector", lambda v: v.tensor_tensor(
                zsb[0:32, :], zbf[0:32, :], sbf[0:32, :], OP.mult))
            mt0 = io0 // 128
            P.do("scalar", lambda sc, mt0=mt0: sc.dma_start(
                d_sgur[:, :].rearrange("m (g l) -> g m l", g=32)[:, mt0:mt0 + 43, :],
                sbf[0:32, :].rearrange("g (m l) -> g m l", m=43)),
                wait=(vt,), dma=True)
            vtok = P.do("scalar", lambda sc, io0=io0: sc.dma_start(
                d_zsgu[:, io0:io0 + 5504], zsb[0:32, :]), wait=(vt,), dma=True)

        w1 = P.do("sync", lambda sy: sy.dma_start(sf[:, 0:4096], d_sdn[:]),
                  wait=(vtok,), dma=True)
        w2 = P.do("sync", lambda sy: sy.dma_start(qzi[:, 0:512], d_qzdn[:]), dma=True)
        P.do("vector", lambda v: v.tensor_copy(sbf[:, 0:4096], sf[:, 0:4096]),
             wait=(w1, vtok))
        for idx in range(8):
            P.do("vector", lambda v, idx=idx: v.tensor_scalar(
                bass.AP(zi, idx, [[5504, 86], [8, 512]]), qzi[:, 0:512],
                4 * idx, 15, OP.logical_shift_right, OP.bitwise_and), wait=(w2,))
        P.do("vector", lambda v: v.tensor_copy(zbf[:, 0:4096], zi[:, 0:4096]))
        vt = P.do("vector", lambda v: v.tensor_tensor(
            zsb[:, 0:4096], zbf[:, 0:4096], sbf[:, 0:4096], OP.mult))
        P.do("scalar", lambda sc: sc.dma_start(
            d_sdnr[:, :].rearrange("m (g l) -> g m l", g=86),
            sbf[:, 0:4096].rearrange("g (m l) -> g m l", m=32)), wait=(vt,), dma=True)
        P.do("scalar", lambda sc: sc.dma_start(d_zsdn[:], zsb[:, 0:4096]),
             wait=(vt,), dma=True)
        P.join()

    if upto == 'p0':
        P.drain_dmas()
        raise _Stop

    with ExitStack() as ph:
        pb = lambda n, sh, dt: ph.enter_context(nc.sbuf_tensor(n, sh, dt))
        psm = lambda n, sh: ph.enter_context(nc.psum_tensor(n, sh, F32))
        ct = pb("ct_s", [128, 4 * S], BF16)
        ln_stack = ExitStack()
        pbl = lambda n, sh, dt: ln_stack.enter_context(nc.sbuf_tensor(n, sh, dt))
        psl = lambda n, sh: ln_stack.enter_context(nc.psum_tensor(n, sh, F32))
        indgu = pbl("indgu_s", [128, 32 * 32], BF16)
        cstf = pbl("p1_cst", [128, 1024], F32)
        xf = pbl("p1_xf", [128, 32 * 512], F32)
        xbf = pbl("p1_xbf", [128, 2 * 512], BF16)
        xsq = pbl("p1_xsq", [128, 2 * 512], BF16)
        xnb = pbl("p1_xnb", [128, 32 * 512], BF16)
        tmpf = pbl("p1_tmp", [128, 512], F32)
        wkvf = pbl("p1_wkvf", [128, 2 * 512], F32)
        wkv = pbl("p1_wkv", [128, 32 * 512], BF16)
        row3 = pbl("p1_row3", [1, 512], F32)
        row4 = pbl("p1_row4", [1, 512], F32)
        p_cat = psl("p1_cat", [128, 4 * 512])
        p_xs = psl("p1_xs", [32, 512])
        p_bc1 = psl("p1_bc1", [128, 512])
        p_bc2 = psl("p1_bc2", [128, 512])
        p_st1 = p_cat[0:1, 0:512]
        p_st2 = p_cat[0:1, 512:1024]

        w = P.do("sync", lambda sy: sy.dma_start(cstf[:], d_indgu[:]), dma=True)
        ind_ok = P.do("vector", lambda v: v.tensor_copy(indgu[:], cstf[:]), wait=(w,))

        ctoks = {}
        for j in range(32):
            w = P.do("sync", lambda sy, j=j: sy.dma_start(
                wkvf[:, (j % 2) * 512:(j % 2 + 1) * 512],
                d_wkvd[j * 128:(j + 1) * 128, :]),
                wait=(ctoks.get(j - 2),), dma=True)
            ctoks[j] = P.do("scalar", lambda sc, j=j: sc.activation(
                wkv[:, j * 512:(j + 1) * 512],
                wkvf[:, (j % 2) * 512:(j % 2 + 1) * 512], AF.Copy), wait=(w,))
        wkv_done = ctoks[31]

        cat_evac = None
        xn_used = None
        for c in range(4):
            tc = c * 512
            xw = []
            for j in range(32):
                xw.append(P.do("sync", lambda sy, j=j, tc=tc: sy.dma_start(
                    xf[:, j * 512:(j + 1) * 512],
                    d_xT[j * 128:(j + 1) * 128, tc:tc + 512]),
                    wait=(xn_used if j == 0 else None,), dma=True))
            mm1 = mm2 = None
            for j in range(32):
                b = (j % 2) * 512
                vt1 = P.do("vector", lambda v, j=j, b=b: v.tensor_copy(
                    xbf[:, b:b + 512], xf[:, j * 512:(j + 1) * 512]),
                    wait=(xw[j], mm1 if j >= 2 else None,
                          cat_evac if j <= 1 else None))
                vt2 = P.do("vector", lambda v, j=j, b=b: v.tensor_tensor(
                    xsq[:, b:b + 512], xbf[:, b:b + 512], xbf[:, b:b + 512], OP.mult))
                mm1 = P.do("tensor", lambda t, j=j, b=b: t.matmul(
                    p_st1, ones128[:], xbf[:, b:b + 512],
                    start=(j == 0), stop=(j == 31)), wait=(vt1,))
                mm2 = P.do("tensor", lambda t, j=j, b=b: t.matmul(
                    p_st2, ones128[:], xsq[:, b:b + 512],
                    start=(j == 0), stop=(j == 31)), wait=(vt2,))
            rv = P.do("vector", lambda v: v.tensor_scalar(
                row1[:], p_st1, 1.0 / D, None, OP.mult), wait=(mm1, mm2))
            P.do("vector", lambda v: v.tensor_scalar(
                row2[:], p_st2, 1.0 / D, None, OP.mult))
            P.do("vector", lambda v: v.tensor_tensor(row3[:], row1[:], row1[:], OP.mult))
            P.do("vector", lambda v: v.tensor_tensor(row2[:], row2[:], row3[:], OP.subtract))
            P.do("vector", lambda v: v.tensor_scalar(row2[:], row2[:], 1e-5, None, OP.add))
            rv = P.do("vector", lambda v: v.reciprocal(row3[:], row2[:]))
            rs = P.do("scalar", lambda sc: sc.activation(row4[:], row3[:], AF.Sqrt),
                      wait=(rv,))
            rv = P.do("vector", lambda v: v.tensor_tensor(
                row2[:], row1[:], row4[:], OP.mult), wait=(rs,))
            bc = P.do("tensor", lambda t: t.matmul(
                p_bc1[:], ones1f[:], row4[:], start=True, stop=True), wait=(rs, rv))
            bc = P.do("tensor", lambda t: t.matmul(
                p_bc2[:], ones1f[:], row2[:], start=True, stop=True), wait=(rv,))
            vx = None
            for j in range(32):
                sl = slice(j * 512, (j + 1) * 512)
                P.do("vector", lambda v, sl=sl: v.tensor_tensor(
                    tmpf[:], xf[:, sl], p_bc1[:], OP.mult), wait=(bc,))
                vx = P.do("vector", lambda v, sl=sl: v.tensor_tensor(
                    xnb[:, sl], tmpf[:], p_bc2[:], OP.subtract))
            mmc = mmx = None
            for j in range(32):
                sl = slice(j * 512, (j + 1) * 512)
                for mt in range(4):
                    mmc = P.do("tensor", lambda t, j=j, mt=mt, sl=sl: t.matmul(
                        p_cat[:, mt * 512:(mt + 1) * 512],
                        wkv[:, j * 512 + mt * 128: j * 512 + (mt + 1) * 128],
                        xnb[:, sl], start=(j == 0), stop=(j == 31)),
                        wait=(vx if j == 0 and mt == 0 else None,
                              wkv_done if j == 0 and mt == 0 else None,
                              cat_evac if j == 0 and mt == 0 else None))
                mmx = P.do("tensor", lambda t, j=j, sl=sl: t.matmul(
                    p_xs[:], indgu[:, j * 32:(j + 1) * 32], xnb[:, sl],
                    start=(j == 0), stop=(j == 31)),
                    wait=(ind_ok if j == 0 and c == 0 else None,))
            for mt in range(4):
                cat_evac = P.do("scalar", lambda sc, mt=mt, tc=tc: sc.activation(
                    ct[:, mt * S + tc: mt * S + tc + 512],
                    p_cat[:, mt * 512:(mt + 1) * 512], AF.Copy), wait=(mmc,))
            cat_evac = P.do("scalar", lambda sc, tc=tc: sc.activation(
                xsumn[:, tc:tc + 512], p_xs[:], AF.Copy), wait=(mmx,))
            xn_used = None
            for j in range(32):
                xn_used = P.do("scalar", lambda sc, j=j, tc=tc: sc.dma_start(
                    d_xnT[j, :, tc:tc + 512], xnb[:, j * 512:(j + 1) * 512]),
                    wait=(vx, mmc), dma=True)
        P.join()
        ln_stack.close()

        if upto != 'p1a':
            with ExitStack() as ph2:
                pb2 = lambda n, sh, dt: ph2.enter_context(nc.sbuf_tensor(n, sh, dt))
                psm2 = lambda n, sh: ph2.enter_context(nc.psum_tensor(n, sh, F32))
                wupf = pb2("p1_wupf", [128, 2 * 1024], F32)
                wku = pb2("p1_wku", [128, 4 * 1024], BF16)
                wvu = pb2("p1_wvu", [128, 4 * 1024], BF16)
                kbuf = pb2("p1_kbuf", [128, 2 * 2048], BF16)
                k2b = pb2("p1_k2", [128, 512], BF16)
                vbuf = pb2("p1_vbuf", [128, 2 * 512], BF16)
                p_kv = psm2("p1_pkv", [128, 4 * 512])
                p_kn = psm2("p1_pkn", [1, 2048])

                toks = {}
                i = 0
                for (dsrc, dst) in [(d_wkup, wku), (d_wvup, wvu)]:
                    for j0 in range(4):
                        b = (i % 2) * 1024
                        w = P.do("sync", lambda sy, dsrc=dsrc, j0=j0, b=b: sy.dma_start(
                            wupf[:, b:b + 1024], dsrc[j0]),
                            wait=(toks.get(i - 2),), dma=True)
                        toks[i] = P.do("scalar", lambda sc, dst=dst, j0=j0, b=b: sc.activation(
                            dst[:, j0 * 1024:(j0 + 1) * 1024], wupf[:, b:b + 1024], AF.Copy),
                            wait=(w,))
                        i += 1
                wup_done = toks[7]

                kev = {}
                kspill = {}
                kst_done = {}
                kn_done = None
                k2_used = None
                for h2 in range(HK):
                    ktok = None
                    for hb in range(8):
                        idx = h2 * 8 + hb
                        bk = idx % 4
                        Bp = 64 * (hb % 2)
                        mm = None
                        for j0 in range(4):
                            mm = P.do("tensor", lambda t, h2=h2, hb=hb, Bp=Bp, j0=j0, bk=bk: t.matmul(
                                p_kv[:, bk * 512: bk * 512 + 256],
                                wku[Bp:Bp + 64, j0 * 1024 + h2 * 128: j0 * 1024 + (h2 + 1) * 128],
                                bass.AP(ct, Bp * (4 * S) + (hb // 2) * S + j0,
                                        [[4 * S, 64], [8, 256]]),
                                start=(j0 == 0), stop=(j0 == 3)),
                                wait=(wup_done if idx == 0 and j0 == 0 else None,
                                      kev.get(idx - 4) if j0 == 0 else None))
                        ev = P.do("scalar", lambda sc, h2=h2, hb=hb, bk=bk: sc.activation(
                            kbuf[:, (h2 % 2) * 2048 + hb * 256:(h2 % 2) * 2048 + (hb + 1) * 256],
                            p_kv[:, bk * 512: bk * 512 + 256], AF.Copy),
                            wait=(mm, kspill.get(h2 - 2) if hb == 0 else None,
                                  kst_done.get(h2 - 2) if hb == 0 else None))
                        kev[idx] = ev
                    qn = None
                    for chk in range(4):
                        v2 = P.do("vector", lambda v, h2=h2, chk=chk: v.tensor_tensor(
                            k2b[:],
                            kbuf[:, (h2 % 2) * 2048 + chk * 512:(h2 % 2) * 2048 + (chk + 1) * 512],
                            kbuf[:, (h2 % 2) * 2048 + chk * 512:(h2 % 2) * 2048 + (chk + 1) * 512],
                            OP.mult),
                            wait=(ev, k2_used if chk == 0 else None))
                        qn = P.do("tensor", lambda t, chk=chk: t.matmul(
                            p_kn[0:1, chk * 512:(chk + 1) * 512], ones128[:], k2b[:],
                            start=True, stop=True),
                            wait=(v2, kn_done if chk == 0 else None))
                        k2_used = qn
                    kst_done[h2] = qn
                    kn_done = P.do("vector", lambda v, h2=h2: v.reduce_max(
                        row1[0:1, h2:h2 + 1], p_kn[0:1, :], mybir.AxisListType.X),
                        wait=(qn,))
                    kspill[h2] = P.do("scalar", lambda sc, h2=h2: sc.dma_start(
                        d_kT[h2], kbuf[:, (h2 % 2) * 2048:(h2 % 2 + 1) * 2048]),
                        wait=(ev,), dma=True)
                sq = P.do("scalar", lambda sc: sc.activation(
                    row2[0:1, 0:8], row1[0:1, 0:8], AF.Sqrt), wait=(kn_done,))
                sqk_done = P.do("vector", lambda v: v.tensor_scalar(
                    sqkn[:], row2[0:1, 0:8], -SAFETY, None, OP.mult), wait=(sq,))

                if upto != 'p1k':
                    vev = {}
                    vspill = {}
                    for it in range(16):
                        hb = it // 2
                        Bp = 64 * (hb % 2)
                        toff = 1024 * (it % 2)
                        for nch in range(2):
                            idx = it * 2 + nch
                            pk = p_kv[:, (idx % 2) * 512:(idx % 2 + 1) * 512]
                            mm = None
                            for j0 in range(4):
                                mm = P.do("tensor", lambda t, hb=hb, Bp=Bp, toff=toff, j0=j0, nch=nch, pk=pk: t.matmul(
                                    pk,
                                    bass.AP(ct, Bp * (4 * S) + (hb // 2) * S + toff + j0 + 4,
                                            [[4 * S, 64], [8, 128]]),
                                    wvu[Bp:Bp + 64, j0 * 1024 + nch * 512: j0 * 1024 + (nch + 1) * 512],
                                    start=(j0 == 0), stop=(j0 == 3)),
                                    wait=(vev.get(idx - 2) if j0 == 0 else None,))
                            ev = P.do("scalar", lambda sc, idx=idx, pk=pk: sc.activation(
                                vbuf[:, (idx % 2) * 512:(idx % 2 + 1) * 512], pk, AF.Copy),
                                wait=(mm, vspill.get(idx - 2)))
                            vev[idx] = ev
                            vspill[idx] = P.do("scalar", lambda sc, it=it, nch=nch, idx=idx: sc.dma_start(
                                d_v[it, :, nch * 512:(nch + 1) * 512],
                                vbuf[:, (idx % 2) * 512:(idx % 2 + 1) * 512]),
                                wait=(ev,), dma=True)
                    P.join()

    if upto in ('p1', 'p1a', 'p1k'):
        P.drain_dmas()
        raise _Stop

    with ExitStack() as ph:
        pb = lambda n, sh, dt: ph.enter_context(nc.sbuf_tensor(n, sh, dt))
        xnr = pb("xnr", [128, 32 * S], BF16)

        xw = []
        for j in range(32):
            xw.append(P.do("sync", lambda sy, j=j: sy.dma_start(
                xnr[:, j * S:(j + 1) * S], d_xnT[j]), dma=True))
        xnr_ready = xw[-1]

        with ExitStack() as p2:
            pb2 = lambda n, sh, dt: p2.enter_context(nc.sbuf_tensor(n, sh, dt))
            psm2 = lambda n, sh: p2.enter_context(nc.psum_tensor(n, sh, F32))
            wqs = pb2("p2_wqs", [128, 2048], F32)
            wqb = pb2("p2_wqb", [128, 2 * 4096], BF16)
            qb = pb2("p2_qb", [128, 2 * S], BF16)
            q2t = pb2("p2_q2", [128, 2 * 512], BF16)
            rowq = pb2("p2_rowq", [1, S], F32)
            p_q = psm2("p2_pq", [128, 4 * 512])
            p_qn = psm2("p2_pqn", [1, 2048])

            cast_done = {}
            ev_done = {}
            q_spill = {}
            n_spill = {}
            qn_last = {}
            vn_last = {}
            for m in range(H + 1):
                if m < H:
                    for half in range(2):
                        w = P.do("sync", lambda sy, m=m, half=half: sy.dma_start(
                            wqs[:], d_wq[m, :, half * 2048:(half + 1) * 2048]),
                            wait=(cast_done.get((m, half - 1)) or cast_done.get((m - 1, 1)),
                                  ev_done.get(m - 2) if half == 0 else None), dma=True)
                        cast_done[(m, half)] = P.do("scalar", lambda sc, m=m, half=half: sc.activation(
                            wqb[:, (m % 2) * 4096 + half * 2048:(m % 2) * 4096 + (half + 1) * 2048],
                            wqs[:], AF.Copy), wait=(w,))
                if m > 0:
                    mp = m - 1
                    mm = None
                    for j in range(32):
                        for nch in range(4):
                            mm = P.do("tensor", lambda t, mp=mp, j=j, nch=nch: t.matmul(
                                p_q[:, nch * 512:(nch + 1) * 512],
                                wqb[:, (mp % 2) * 4096 + j * 128:(mp % 2) * 4096 + (j + 1) * 128],
                                xnr[:, j * S + nch * 512: j * S + (nch + 1) * 512],
                                start=(j == 0), stop=(j == 31)),
                                wait=(cast_done[(mp, 1)] if j == 0 and nch == 0 else None,
                                      xnr_ready if j == 0 and nch == 0 else None,
                                      ev_done.get(mp - 1) if j == 0 and nch == 0 else None))
                    ev = P.do("scalar", lambda sc, mp=mp: sc.activation(
                        qb[:, (mp % 2) * S:(mp % 2 + 1) * S], p_q[:], AF.Copy, scale=SCALE),
                        wait=(mm, q_spill.get(mp - 2), qn_last.get(mp - 2)))
                    ev_done[mp] = ev
                    q_spill[mp] = P.do("scalar", lambda sc, mp=mp: sc.dma_start(
                        d_qT[mp], qb[:, (mp % 2) * S:(mp % 2 + 1) * S]),
                        wait=(ev,), dma=True)
                    qn = None
                    for nch in range(4):
                        bq = (nch % 2) * 512
                        v2 = P.do("vector", lambda v, mp=mp, nch=nch, bq=bq: v.tensor_tensor(
                            q2t[:, bq:bq + 512],
                            qb[:, (mp % 2) * S + nch * 512:(mp % 2) * S + (nch + 1) * 512],
                            qb[:, (mp % 2) * S + nch * 512:(mp % 2) * S + (nch + 1) * 512],
                            OP.mult),
                            wait=(ev if nch == 0 else None,
                                  qn_last.get(mp - 1) if nch == 0 else None,
                                  qn if nch >= 2 else None))
                        qn = P.do("tensor", lambda t, nch=nch, bq=bq: t.matmul(
                            p_qn[0:1, nch * 512:(nch + 1) * 512], ones128[:],
                            q2t[:, bq:bq + 512], start=True, stop=True),
                            wait=(v2, (n_spill.get(mp - 1) or sqk_done) if nch == 0 else None))
                    qn_last[mp] = qn
                    sq = P.do("scalar", lambda sc: sc.activation(
                        rowq[:], p_qn[0:1, :], AF.Sqrt),
                        wait=(qn, vn_last.get(mp - 1), sqk_done if mp == 0 else None))
                    vn = P.do("vector", lambda v: v.reduce_max(
                        row2[0:1, 9:10], rowq[:], mybir.AxisListType.X), wait=(sq,))
                    vn = P.do("vector", lambda v, mp=mp: v.tensor_scalar(
                        c0row[0:1, mp:mp + 1], row2[0:1, 9:10],
                        sqkn[0:1, mp // 4:mp // 4 + 1], None, OP.mult), wait=(vn,))
                    vn_last[mp] = vn
                    n_spill[mp] = vn
            jt = P.join()
            global _c0_done_tok
            _c0_done_tok = jt.get("vector")

        with ExitStack() as p3:
            pb3 = lambda n, sh, dt: p3.enter_context(nc.sbuf_tensor(n, sh, dt))
            psm3 = lambda n, sh: p3.enter_context(nc.psum_tensor(n, sh, F32))
            qwb = pb3("p3_qw", [128, 512], I32)
            nib = pb3("p3_nib", [128, 2048], I32)
            w1g = pb3("p3_w1g", [128, 2 * 4096], BF16)
            w1u = pb3("p3_w1u", [128, 2 * 4096], BF16)
            srow = pb3("p3_srow", [1, 4096], BF16)
            zsl = pb3("p3_zsl", [32, 4 * 128], BF16)
            hg = pb3("p3_hg", [128, S], BF16)
            hu = pb3("p3_hu", [128, S], BF16)
            hbb = pb3("p3_hb", [128, S], BF16)
            p_acc = psm3("p3_acc", [128, 4 * 512])
            p_bc = psm3("p3_bc", [128, 4 * 512])

            NP = DN_K
            bank_last = [None]
            nib_free = [None]
            deq_io = {}
            deq_done = {}
            acc_tok = {}
            unp_done = {}
            bc_done = {}
            ev_done = {}
            hmul_done = {}
            hdma = {}

            def deq_wave(pair, gu, wave):
                m = pair + NP * gu
                slot_t = {}
                if wave == 0:
                    wq = P.do("sync", lambda sy, m=m: sy.dma_start(qwb[:], d_qwgu[m]),
                              wait=(unp_done.get((pair, gu - 1) if gu == 1 else (pair - 1, 1)),), dma=True)
                    ws = P.do("sync", lambda sy, m=m: sy.dma_start(srow[:], d_sgur[m:m + 1, :]),
                              wait=(bc_done.get((pair, gu - 1) if gu == 1 else (pair - 1, 1)),), dma=True)
                    zslot = (2 * pair + gu) % 4
                    wz = P.do("sync", lambda sy, m=m, zslot=zslot: sy.dma_start(
                        zsl[:, zslot * 128:(zslot + 1) * 128], d_zsgu[:, m * 128:(m + 1) * 128]),
                        wait=(acc_tok.get((pair - 2, gu)),), dma=True)
                    deq_io[(pair, gu)] = (wq, ws, wz)
                wq, ws, wz = deq_io[(pair, gu)]
                w1 = w1g if gu == 0 else w1u
                wbase = (pair % 2) * 4096
                un = None
                for idx in range(8):
                    un = P.do("vector", lambda v, idx=idx, wave=wave: v.tensor_scalar(
                        bass.AP(nib, idx, [[2048, 128], [8, 256]]),
                        qwb[:, wave * 256:(wave + 1) * 256],
                        4 * idx, 15, OP.logical_shift_right, OP.bitwise_and),
                        wait=(wq if idx == 0 else None,
                              nib_free[0] if idx == 0 else None))
                if wave == 1:
                    unp_done[(pair, gu)] = un
                bcs = None
                for q4 in range(4):
                    q = wave * 4 + q4
                    bcs = P.do("tensor", lambda t, q=q, q4=q4: t.matmul(
                        p_bc[:, q4 * 512:(q4 + 1) * 512], ones1[:],
                        srow[:, q * 512:(q + 1) * 512], start=True, stop=True),
                        wait=(ws if q4 == 0 else None,
                              bank_last[0] if q4 == 0 else None))
                tt = P.do("vector", lambda v, wave=wave, w1=w1, wbase=wbase: v.tensor_tensor(
                    w1[:, wbase + wave * 2048: wbase + (wave + 1) * 2048],
                    nib[:], p_bc[:], OP.mult),
                    wait=(bcs, un, acc_tok.get((pair - 2, gu)) if wave == 0 else None))
                bank_last[0] = tt
                nib_free[0] = tt
                if wave == 1:
                    bc_done[(pair, gu)] = tt
                    deq_done[(pair, gu)] = (tt, wz)

            def acc(pair, gu):
                zslot = (2 * pair + gu) % 4
                w1 = w1g if gu == 0 else w1u
                wbase = (pair % 2) * 4096
                dq, wz = deq_done[(pair, gu)]
                mm = None
                for j in range(32):
                    for nch in range(4):
                        mm = P.do("tensor", lambda t, j=j, nch=nch, w1=w1, wbase=wbase: t.matmul(
                            p_acc[:, nch * 512:(nch + 1) * 512],
                            w1[:, wbase + j * 128: wbase + (j + 1) * 128],
                            xnr[:, j * S + nch * 512: j * S + (nch + 1) * 512],
                            start=(j == 0), stop=False),
                            wait=(dq if j == 0 and nch == 0 else None,
                                  ev_done.get((pair, gu - 1) if gu == 1 else (pair - 1, 1)) if j == 0 and nch == 0 else None))
                for nch in range(4):
                    mm = P.do("tensor", lambda t, nch=nch, zslot=zslot: t.matmul(
                        p_acc[:, nch * 512:(nch + 1) * 512],
                        zsl[:, zslot * 128:(zslot + 1) * 128],
                        xsumn[:, nch * 512:(nch + 1) * 512], start=False, stop=(nch == 3)),
                        wait=(wz if nch == 0 else None,))
                acc_tok[(pair, gu)] = mm
                dst = hg if gu == 0 else hu
                fn = AF.Silu if gu == 0 else AF.Copy
                ev_done[(pair, gu)] = P.do("scalar", lambda sc, dst=dst, fn=fn: sc.activation(
                    dst[:], p_acc[:], fn),
                    wait=(mm, hmul_done.get(pair - 1) if gu == 0 else None))
                if gu == 1:
                    hm = P.do("vector", lambda v: v.tensor_tensor(
                        hbb[:], hg[:], hu[:], OP.mult),
                        wait=(ev_done[(pair, 0)], ev_done[(pair, 1)], hdma.get(pair - 1)))
                    hmul_done[pair] = hm
                    hdma[pair] = P.do("scalar", lambda sc, pair=pair: sc.dma_start(
                        d_hT[pair], hbb[:]), wait=(hm,), dma=True)

            deq_wave(0, 0, 0)
            deq_wave(0, 0, 1)
            deq_wave(0, 1, 0)
            deq_wave(0, 1, 1)
            for pair in range(NP):
                if pair + 1 < NP:
                    deq_wave(pair + 1, 0, 0)
                acc(pair, 0)
                if pair + 1 < NP:
                    deq_wave(pair + 1, 0, 1)
                    deq_wave(pair + 1, 1, 0)
                acc(pair, 1)
                if pair + 1 < NP:
                    deq_wave(pair + 1, 1, 1)
            P.join()

    if upto == 'p3':
        P.drain_dmas()
        raise _Stop

    with ExitStack() as ph:
        pb = lambda n, sh, dt: ph.enter_context(nc.sbuf_tensor(n, sh, dt))
        psm = lambda n, sh: ph.enter_context(nc.psum_tensor(n, sh, F32))
        qdb = pb("b_qd", [128, 2 * 1376], I32)
        nibd = pb("b_nib", [128, 11008], I32)
        nibb = pb("b_nibb", [128, 11008], BF16)
        w1d = pb("b_w1d", [128, 2 * 11008], BF16)
        sdr = pb("b_sdr", [1, 2 * 11008], BF16)
        p_bc = psm("b_bc", [128, 2 * 512])

        unp_prev = {}
        tt_prev = {}
        dma_prev = {}
        bank_last = [None, None]
        for m in range(32):
            b = (m % 2)
            wq = P.do("sync", lambda sy, m=m, b=b: sy.dma_start(
                qdb[:, b * 1376:(b + 1) * 1376], d_qwdn[m]),
                wait=(unp_prev.get(m - 2),), dma=True)
            ws = P.do("sync", lambda sy, m=m, b=b: sy.dma_start(
                sdr[:, b * 11008:(b + 1) * 11008], d_sdnr[m:m + 1, :]),
                wait=(tt_prev.get(m - 2),), dma=True)
            un = None
            for idx in range(8):
                un = P.do("vector", lambda v, idx=idx, b=b: v.tensor_scalar(
                    bass.AP(nibd, idx, [[11008, 128], [8, 1376]]),
                    qdb[:, b * 1376:(b + 1) * 1376],
                    4 * idx, 15, OP.logical_shift_right, OP.bitwise_and),
                    wait=(wq if idx == 0 else None,
                          tt_prev.get(m - 1) if idx == 0 else None))
            unp_prev[m] = un
            cst = P.do("vector", lambda v: v.tensor_copy(nibb[:], nibd[:]), wait=(un,))
            tt = None
            for q in range(22):
                n = min(512, 11008 - q * 512)
                bk = q % 2
                bc = P.do("tensor", lambda t, q=q, n=n, bk=bk, b=b: t.matmul(
                    p_bc[:, bk * 512:bk * 512 + n], ones1[:],
                    sdr[:, b * 11008 + q * 512: b * 11008 + q * 512 + n],
                    start=True, stop=True), wait=(ws if q == 0 else None, bank_last[bk]))
                tt = P.do("vector", lambda v, q=q, n=n, bk=bk, b=b: v.tensor_tensor(
                    w1d[:, b * 11008 + q * 512: b * 11008 + q * 512 + n],
                    nibb[:, q * 512:q * 512 + n],
                    p_bc[:, bk * 512:bk * 512 + n], OP.mult),
                    wait=(bc, cst if q == 0 else None,
                          dma_prev.get(m - 2) if q == 0 else None))
                bank_last[bk] = tt
            tt_prev[m] = tt
            dma_prev[m] = P.do("scalar", lambda sc, m=m, b=b: sc.dma_start(
                d_w1dn[m], w1d[:, b * 11008:(b + 1) * 11008]), wait=(tt,), dma=True)
        P.join()

    if upto == 'p3b':
        P.drain_dmas()
        raise _Stop

    with ExitStack() as ph:
        pb = lambda n, sh, dt: ph.enter_context(nc.sbuf_tensor(n, sh, dt))
        psm = lambda n, sh: ph.enter_context(nc.psum_tensor(n, sh, F32))
        inddn = pb("p4_ind", [128, 86 * 86], BF16)
        hc = pb("p4_hc", [128, 86 * 512], BF16)
        w1p = pb("p4_w1p", [128, 2 * 11008], BF16)
        zdn = pb("p4_zdn", [86, 4096], BF16)
        hsn = pb("p4_hsn", [86, 512], BF16)
        gb = pb("p4_gb", [128, 2 * 512], BF16)
        p_dn = psm("p4_dn", [128, 2 * 512])
        p_hs = psm("p4_hs", [86, 512])

        with ExitStack() as ph2:
            cstf = ph2.enter_context(nc.sbuf_tensor("p4_cst", [128, 86 * 86], F32))
            w = P.do("sync", lambda sy: sy.dma_start(cstf[:], d_inddn[:]), dma=True)
            ind_ok = P.do("vector", lambda v: v.tensor_copy(inddn[:], cstf[:]), wait=(w,))
            P.flush()
        wz = P.do("sync", lambda sy: sy.dma_start(zdn[:], d_zsdn[:]), dma=True)

        panel_used = {}
        g_spill = {}
        ev_tok = {}
        hs_prev = None
        for c in range(4):
            tc = c * 512
            hw = None
            for g in range(DN_K):
                hw = P.do("sync", lambda sy, g=g, tc=tc: sy.dma_start(
                    hc[:, g * 512:(g + 1) * 512], d_hT[g, :, tc:tc + 512]),
                    wait=(panel_used.get((c - 1, 31)) if g == 0 else None,), dma=True)
            hs = None
            for g in range(DN_K):
                hs = P.do("tensor", lambda t, g=g: t.matmul(
                    p_hs[:], inddn[:, g * 86:(g + 1) * 86], hc[:, g * 512:(g + 1) * 512],
                    start=(g == 0), stop=(g == DN_K - 1)),
                    wait=(hw if g == 0 else None,
                          ind_ok if g == 0 and c == 0 else None,
                          hs_prev if g == 0 else None))
            hs = P.do("scalar", lambda sc: sc.activation(hsn[:], p_hs[:], AF.Copy),
                      wait=(hs,))
            hs_prev = hs
            for m in range(32):
                b = m % 2
                wp = P.do("sync", lambda sy, m=m, b=b: sy.dma_start(
                    w1p[:, b * 11008:(b + 1) * 11008], d_w1dn[m]),
                    wait=(panel_used.get((c, m - 2), panel_used.get((c - 1, 30 + m)) if c > 0 else None),), dma=True)
                mm = None
                for g in range(DN_K):
                    mm = P.do("tensor", lambda t, g=g, b=b: t.matmul(
                        p_dn[:, b * 512:(b + 1) * 512],
                        w1p[:, b * 11008 + g * 128: b * 11008 + (g + 1) * 128],
                        hc[:, g * 512:(g + 1) * 512], start=(g == 0), stop=False),
                        wait=(wp if g == 0 else None, hs if g == 0 else None,
                              ev_tok.get((c, m - 2), ev_tok.get((c - 1, 30 + m)) if c > 0 else None) if g == 0 else None))
                mm = P.do("tensor", lambda t, m=m, b=b: t.matmul(
                    p_dn[:, b * 512:(b + 1) * 512], zdn[:, m * 128:(m + 1) * 128], hsn[:],
                    start=False, stop=True), wait=(wz if m == 0 and c == 0 else None,))
                panel_used[(c, m)] = mm
                ev = P.do("scalar", lambda sc, b=b: sc.activation(
                    gb[:, b * 512:(b + 1) * 512], p_dn[:, b * 512:(b + 1) * 512], AF.Silu),
                    wait=(mm, g_spill.get((c, m - 2), g_spill.get((c - 1, 30 + m)) if c > 0 else None)))
                ev_tok[(c, m)] = ev
                g_spill[(c, m)] = P.do("scalar", lambda sc, m=m, tc=tc, b=b: sc.dma_start(
                    d_gT[m, :, tc:tc + 512], gb[:, b * 512:(b + 1) * 512]),
                    wait=(ev,), dma=True)
        P.join()

    if upto == 'p4':
        P.drain_dmas()
        raise _Stop

    with ExitStack() as ph:
        pb = lambda n, sh, dt: ph.enter_context(nc.sbuf_tensor(n, sh, dt))
        psm = lambda n, sh: ph.enter_context(nc.psum_tensor(n, sh, F32))
        mtri = pb("p5_mtri", [128, 128], BF16)
        kTr = pb("p5_kT", [128, HK * S], BF16)
        vNr = pb("p5_vN", [128, 16 * HK * HD], BF16)
        qbr = pb("p5_qb", [128, 2 * S], BF16)
        negcb = pb("p5_ngb", [128, 2], F32)
        ptil = pb("p5_pt", [128, 3 * 512], BF16)
        ptmp = pb("p5_ptm", [128, 128], BF16)
        recs = pb("p5_rec", [128, 2 * 512], F32)
        otb = pb("p5_otb", [128, 2 * 512], BF16)
        p_st = psm("p5_st", [128, 2 * 512])
        p_cs = psm("p5_cs", [1, 2 * 512])
        p_ot = psm("p5_ot", [128, 2 * 512])
        p_rb = psm("p5_rb", [128, 2 * 512])

        with ExitStack() as ph2:
            cstf = ph2.enter_context(nc.sbuf_tensor("p5_cst", [128, 128], F32))
            w = P.do("sync", lambda sy: sy.dma_start(cstf[:], d_mtri[:]), dma=True)
            P.do("vector", lambda v: v.tensor_copy(mtri[:], cstf[:]), wait=(w,))
            P.flush()
        for h2 in range(HK):
            P.do("sync", lambda sy, h2=h2: sy.dma_start(
                kTr[:, h2 * S:(h2 + 1) * S], d_kT[h2]), dma=True)
        kv_ok = None
        for it in range(16):
            kv_ok = P.do("sync", lambda sy, it=it: sy.dma_start(
                vNr[:, it * 1024:(it + 1) * 1024], d_v[it]), dma=True)

        c0_done = _c0_done_tok
        q_used = {}
        st_last = [None, None]
        exp_done = [None, None, None]
        pt_last = [None, None, None]
        ptmp_last = None
        ot_used = [None, None]
        cs_used = [None, None]
        rb_used = [None, None]
        sp_used = [None, None]
        ng_evac = None
        rb_used_row = None
        for h in range(H):
            h2 = h // 4
            bh = (h % 2)
            wq = P.do("sync", lambda sy, h=h, bh=bh: sy.dma_start(
                qbr[:, bh * S:(bh + 1) * S], d_qT[h]), wait=(q_used.get(h - 2),), dma=True)
            nb = P.do("tensor", lambda t, h=h: t.matmul(
                p_rb[:, (h % 2) * 512:(h % 2) * 512 + 1], ones1f[:],
                c0row[0:1, h:h + 1], start=True, stop=True),
                wait=(rb_used[h % 2], c0_done if h == 0 else None))
            ng_evac = P.do("scalar", lambda sc, h=h: sc.activation(
                negcb[:, (h % 2):(h % 2) + 1], p_rb[:, (h % 2) * 512:(h % 2) * 512 + 1],
                AF.Copy), wait=(nb, ng_evac))
            rb_used[h % 2] = ng_evac
            for c in range(4):
                pp = (h * 4 + c) % 2
                ntile = 4 * c + 4
                pend = []
                cs = av = None
                for tt in range(ntile + 2):
                    if tt < ntile:
                        dj = tt - 4 * c
                        off = max(0, dj) * 128
                        qcol = bh * S + 512 * c
                        sb2 = tt % 2
                        sb3 = tt % 3
                        sc_mm = P.do("tensor", lambda t, sb2=sb2, qcol=qcol, h2=h2, tt=tt: t.matmul(
                            p_st[:, sb2 * 512:(sb2 + 1) * 512],
                            kTr[:, h2 * S + tt * 128: h2 * S + (tt + 1) * 128],
                            qbr[0:128, qcol:qcol + 512], start=True, stop=True),
                            wait=(wq if tt == 0 and c == 0 else None,
                                  kv_ok if h == 0 and c == 0 and tt == 0 else None,
                                  st_last[sb2]))
                    if (tt >= 2 and pend) or (tt >= ntile and pend):
                        psb2, ptt = pend.pop(0)
                        cs = P.do("tensor", lambda t, psb2=psb2, pp=pp, ptt=ptt, ntile=ntile: t.matmul(
                            p_cs[0:1, pp * 512:(pp + 1) * 512], ones128[:],
                            ptil[:, psb2 * 512:(psb2 + 1) * 512],
                            start=(ptt == 0), stop=(ptt == ntile - 1)),
                            wait=(exp_done[ptt % 3], cs_used[pp] if ptt == 0 else None))
                        av = P.do("tensor", lambda t, psb2=psb2, pp=pp, ptt=ptt, ntile=ntile, h2=h2: t.matmul(
                            p_ot[:, pp * 512:(pp + 1) * 512],
                            vNr[:, ptt * 1024 + h2 * 128: ptt * 1024 + (h2 + 1) * 128],
                            ptil[:, psb2 * 512:(psb2 + 1) * 512],
                            start=(ptt == 0), stop=(ptt == ntile - 1)),
                            wait=(ot_used[pp] if ptt == 0 else None,))
                        pt_last[psb2] = av
                    if tt >= ntile:
                        continue
                    if dj >= 0:
                        ex2 = None
                        if off + 128 < 512:
                            ex2 = P.do("scalar", lambda sc, sb2=sb2, sb3=sb3, off=off, h=h: sc.activation(
                                ptil[:, sb3 * 512 + off + 128:(sb3 + 1) * 512],
                                p_st[:, sb2 * 512 + off + 128:(sb2 + 1) * 512], AF.Exp,
                                bias=negcb[:, (h % 2):(h % 2) + 1]),
                                wait=(sc_mm, pt_last[sb3], ng_evac))
                        e2 = P.do("scalar", lambda sc, sb2=sb2, off=off, h=h: sc.activation(
                            ptmp[:], p_st[:, sb2 * 512 + off: sb2 * 512 + off + 128], AF.Exp,
                            bias=negcb[:, (h % 2):(h % 2) + 1]),
                            wait=(sc_mm, ptmp_last, ng_evac,
                                  pt_last[sb3] if ex2 is None else None))
                        ex = P.do("vector", lambda v, sb3=sb3, off=off: v.tensor_tensor(
                            ptil[:, sb3 * 512 + off: sb3 * 512 + off + 128], ptmp[:],
                            mtri[:], OP.mult), wait=(e2, ex2))
                        ptmp_last = ex
                        st_last[sb2] = e2
                        if off > 0:
                            ex = P.do("vector", lambda v, sb3=sb3, off=off: v.memset(
                                ptil[:, sb3 * 512: sb3 * 512 + off], 0.0),
                                wait=(pt_last[sb3],))
                        exp_done[sb3] = ex
                    else:
                        ex = P.do("scalar", lambda sc, sb2=sb2, sb3=sb3, h=h: sc.activation(
                            ptil[:, sb3 * 512:(sb3 + 1) * 512],
                            p_st[:, sb2 * 512:(sb2 + 1) * 512], AF.Exp,
                            bias=negcb[:, (h % 2):(h % 2) + 1]),
                            wait=(sc_mm, pt_last[sb3], ng_evac))
                        st_last[sb2] = ex
                        exp_done[sb3] = ex
                    pend.append((sb3, tt))
                rc = P.do("scalar", lambda sc, pp=pp: sc.activation(
                    row1[:], p_cs[0:1, pp * 512:(pp + 1) * 512], AF.Ln),
                    wait=(cs, rb_used_row,))
                cs_used[pp] = rc
                rb = P.do("tensor", lambda t, pp=pp: t.matmul(
                    p_rb[:, pp * 512:(pp + 1) * 512], ones1f[:], row1[:],
                    start=True, stop=True), wait=(rc, rb_used[pp]))
                rb_used_row = rb
                re = P.do("scalar", lambda sc, pp=pp: sc.activation(
                    recs[:, pp * 512:(pp + 1) * 512], p_rb[:, pp * 512:(pp + 1) * 512],
                    AF.Exp, scale=-1.0), wait=(rb,))
                rb_used[pp] = re
                ov = P.do("vector", lambda v, pp=pp: v.tensor_tensor(
                    otb[:, pp * 512:(pp + 1) * 512], p_ot[:, pp * 512:(pp + 1) * 512],
                    recs[:, pp * 512:(pp + 1) * 512], OP.mult),
                    wait=(re, av, sp_used[pp]))
                ot_used[pp] = ov
                sp = P.do("scalar", lambda sc, h=h, c=c, pp=pp: sc.dma_start(
                    d_oT[h, :, c * 512:(c + 1) * 512], otb[:, pp * 512:(pp + 1) * 512]),
                    wait=(ov,), dma=True)
                sp_used[pp] = sp
                if c == 3:
                    q_used[h] = sp
        P.join()

    if upto == 'p5':
        P.drain_dmas()
        raise _Stop

    with ExitStack() as ph:
        pb = lambda n, sh, dt: ph.enter_context(nc.sbuf_tensor(n, sh, dt))
        psm = lambda n, sh: ph.enter_context(nc.psum_tensor(n, sh, F32))
        otr = pb("p6_ot", [128, 32 * S], BF16)
        wos = pb("p6_wos", [128, 2048], F32)
        wob = pb("p6_wob", [128, 2 * 4096], BF16)
        gtb = pb("p6_gt", [128, 2 * S], BF16)
        xtb = pb("p6_xt", [128, S], F32)
        tm6 = pb("p6_tmp", [128, 512], F32)
        ob = pb("p6_ob", [128, S], F32)
        p_s = psm("p6_ps", [128, 2 * 2048])

        ow = []
        for j in range(32):
            ow.append(P.do("sync", lambda sy, j=j: sy.dma_start(
                otr[:, j * S:(j + 1) * S], d_oT[j]), dma=True))
        ot_ready = ow[-1]

        cast_done = {}
        g_done = {}
        x_done = {}
        acc_done = {}
        out_dma = {}
        vv_done = {}
        for m in range(33):
            if m < 32:
                for half in range(2):
                    w = P.do("sync", lambda sy, m=m, half=half: sy.dma_start(
                        wos[:], d_wo[m, :, half * 2048:(half + 1) * 2048]),
                        wait=(cast_done.get((m, half - 1)) or cast_done.get((m - 1, 1)),
                              acc_done.get(m - 2) if half == 0 else None), dma=True)
                    cast_done[(m, half)] = P.do("scalar", lambda sc, m=m, half=half: sc.activation(
                        wob[:, (m % 2) * 4096 + half * 2048:(m % 2) * 4096 + (half + 1) * 2048],
                        wos[:], AF.Copy), wait=(w,))
                g_done[m] = P.do("sync", lambda sy, m=m: sy.dma_start(
                    gtb[:, (m % 2) * S:(m % 2 + 1) * S], d_gT[m]),
                    wait=(vv_done.get(m - 2),), dma=True)
            if m > 0:
                mp = m - 1
                x_done[mp] = P.do("sync", lambda sy, mp=mp: sy.dma_start(
                    xtb[:], d_xT[mp * 128:(mp + 1) * 128, :]),
                    wait=(vv_done.get(mp - 1),), dma=True)
                mm = None
                for j in range(32):
                    for nch in range(4):
                        mm = P.do("tensor", lambda t, mp=mp, j=j, nch=nch: t.matmul(
                            p_s[:, (mp % 2) * 2048 + nch * 512:(mp % 2) * 2048 + (nch + 1) * 512],
                            wob[:, (mp % 2) * 4096 + j * 128:(mp % 2) * 4096 + (j + 1) * 128],
                            otr[:, j * S + nch * 512: j * S + (nch + 1) * 512],
                            start=(j == 0), stop=(j == 31)),
                            wait=(cast_done[(mp, 1)] if j == 0 and nch == 0 else None,
                                  ot_ready if j == 0 and nch == 0 else None,
                                  vv_done.get(mp - 2) if j == 0 and nch == 0 else None))
                acc_done[mp] = mm
                vv = None
                for nch in range(4):
                    sl = slice(nch * 512, (nch + 1) * 512)
                    P.do("vector", lambda v, mp=mp, nch=nch, sl=sl: v.tensor_tensor(
                        tm6[:], p_s[:, (mp % 2) * 2048 + nch * 512:(mp % 2) * 2048 + (nch + 1) * 512],
                        gtb[:, (mp % 2) * S + nch * 512:(mp % 2) * S + (nch + 1) * 512],
                        OP.mult),
                        wait=(mm if nch == 0 else None,
                              g_done[mp] if nch == 0 else None,
                              x_done[mp] if nch == 0 else None,
                              out_dma.get(mp - 1) if nch == 0 else None))
                    vv = P.do("vector", lambda v, sl=sl: v.tensor_tensor(
                        ob[:, sl], tm6[:], xtb[:, sl], OP.add))
                vv_done[mp] = vv
                out_dma[mp] = P.do("scalar", lambda sc, mp=mp: sc.dma_start(
                    d_out[mp * 128:(mp + 1) * 128, :], ob[:]), wait=(vv,), dma=True)
        P.do("scalar", lambda sc: sc.nop(), wait=(out_dma[31],))
        P.join()
    P.drain_dmas()


def _prep_shared(Wq, Wk_down, Wv_down, Wk_up, Wv_up, Wo,
                 qweight_gu, qzeros_gu, scales_gu, qweight_dn, qzeros_dn,
                 scales_dn):
    c = np.ascontiguousarray
    wq_p = c(Wq.T.reshape(32, 128, 32, 128).transpose(2, 1, 0, 3).reshape(H, 128, 4096))
    wo_p = c(Wo.T.reshape(32, 128, 32, 128).transpose(2, 1, 0, 3).reshape(32, 128, 4096))
    wkvd = np.empty((512, D), np.float32)
    for hh in range(8):
        wkvd[64 * hh: 64 * hh + 32] = Wk_down[32 * hh: 32 * hh + 32]
        wkvd[64 * hh + 32: 64 * hh + 64] = Wv_down[32 * hh: 32 * hh + 32]
    wkvd = c(wkvd.T)
    wkupT = Wk_up.T
    wvupT = Wv_up.T
    wkup4 = np.empty((4, 128, 1024), np.float32)
    wvup4 = np.empty((4, 128, 1024), np.float32)
    for j0 in range(4):
        wkup4[j0, 0:64] = wkup4[j0, 64:128] = wkupT[64 * j0: 64 * j0 + 64]
        wvup4[j0, 0:64] = wvup4[j0, 64:128] = wvupT[64 * j0: 64 * j0 + 64]
    qwgu_p = c(qweight_gu.reshape(32, 128, GU_M, 16).transpose(2, 1, 0, 3)
               .reshape(GU_M, 128, 512))
    qwdn_p = c(qweight_dn.reshape(86, 128, 32, 16).transpose(2, 1, 0, 3)
               .reshape(32, 128, 1376))
    indgu = np.zeros((128, 32 * 32), np.float32)
    for j in range(32):
        indgu[:, j * 32 + j] = -1.0
    inddn = np.zeros((128, 86 * 86), np.float32)
    for g in range(86):
        inddn[:, g * 86 + g] = -1.0
    mtri = np.triu(np.ones((128, 128), np.float32))
    return {
        "wq_p": wq_p, "wo_p": wo_p, "wkvd": wkvd, "wkup4": wkup4, "wvup4": wvup4,
        "qwgu_p": qwgu_p, "qzgu": c(qzeros_gu), "sgu": c(scales_gu),
        "qwdn_p": qwdn_p, "qzdn": c(qzeros_dn), "sdn": c(scales_dn),
        "indgu": indgu, "inddn": inddn, "mtri": mtri,
    }


def kernel(L0_attn_hidden_states, L0_attn_Wq, L0_attn_Wk_down, L0_attn_Wv_down,
           L0_attn_Wk_up, L0_attn_Wv_up, L0_attn_Wo, L0_ffn_qweight_gu,
           L0_ffn_qzeros_gu, L0_ffn_scales_gu, L0_ffn_qweight_dn,
           L0_ffn_qzeros_dn, L0_ffn_scales_dn):
    shared = _prep_shared(
        np.asarray(L0_attn_Wq), np.asarray(L0_attn_Wk_down),
        np.asarray(L0_attn_Wv_down), np.asarray(L0_attn_Wk_up),
        np.asarray(L0_attn_Wv_up), np.asarray(L0_attn_Wo),
        np.asarray(L0_ffn_qweight_gu), np.asarray(L0_ffn_qzeros_gu),
        np.asarray(L0_ffn_scales_gu), np.asarray(L0_ffn_qweight_dn),
        np.asarray(L0_ffn_qzeros_dn), np.asarray(L0_ffn_scales_dn))
    x = np.asarray(L0_attn_hidden_states)
    nc = build_program()
    in_maps = []
    for b in range(NCORE):
        m = dict(shared)
        m["xT"] = np.ascontiguousarray(x[b].T)
        in_maps.append(m)
    res = run_bass_kernel_spmd(nc, in_maps, core_ids=list(range(NCORE)))
    out = np.stack([np.asarray(res.results[b]["outT"]).T for b in range(NCORE)])
    return out.astype(np.float32)


# version 05
# speedup vs baseline: 1.1270x; 1.1270x over previous
import numpy as np
from contextlib import ExitStack

import concourse.bass as bass
import concourse.mybir as mybir
from concourse.bass_utils import run_bass_kernel_spmd
from concourse.alu_op_type import AluOpType as OP

F32 = mybir.dt.float32
BF16 = mybir.dt.bfloat16
I32 = mybir.dt.int32
AF = mybir.ActivationFunctionType

NCORE = 8
S = 2048
D = 4096
H = 32
HK = 8
HD = 128
GU_M = 172
DN_K = 86
SCALE = 1.0 / float(np.sqrt(HD))
SAFETY = 1.05

ENGINES = ("tensor", "vector", "scalar", "sync")


class _Stop(Exception):
    pass


class Prog:

    NLANE = 8

    def __init__(self, nc, es):
        self.nc = nc
        self.blk = es.enter_context(nc.Block())
        self.comp_sem = {e: es.enter_context(nc.semaphore(f"c_{e}")) for e in ENGINES}
        self.dma_sem = {e: [es.enter_context(nc.semaphore(f"d_{e}{i}"))
                            for i in range(self.NLANE)]
                        for e in ("sync", "scalar", "vector", "tensor")}
        self.dma_count = {e: 0 for e in self.dma_sem}
        self.ops = []
        self.n_ops = {e: 0 for e in ENGINES}
        self.pending_base = {e: 0 for e in ENGINES}
        self.awaited = set()
        self.resolved = {}
        self.comp_emitted = {e: 0 for e in ENGINES}
        self.last_tok = {e: None for e in ENGINES}

    def do(self, eng, fn, wait=(), dma=False):
        tok = (eng, self.pending_base[eng] + self.n_ops[eng], bool(dma))
        self.n_ops[eng] += 1
        waits = [w for w in wait if w is not None]
        for w in waits:
            if w not in self.resolved:
                self.awaited.add(w)
        self.ops.append((eng, fn, waits, bool(dma), tok))
        self.last_tok[eng] = tok
        return tok

    def flush(self):
        if not self.ops:
            return
        for e in ENGINES:
            t = self.last_tok[e]
            if t is not None and t not in self.resolved:
                self.awaited.add(t)
        per_eng = {e: [] for e in ENGINES}
        for (eng, fn, waits, is_dma, tok) in self.ops:
            per_eng[eng].append((fn, waits, is_dma, tok))
        for e in ENGINES:
            ndma = self.dma_count[e]
            ncomp = self.comp_emitted[e]
            for (fn, waits, is_dma, tok) in per_eng[e]:
                if is_dma:
                    lane = ndma % self.NLANE
                    self.resolved[tok] = (self.dma_sem[e][lane],
                                          16 * (ndma // self.NLANE + 1))
                    ndma += 1
                elif tok in self.awaited:
                    ncomp += 1
                    self.resolved[tok] = (self.comp_sem[e], ncomp)
        for e in ENGINES:
            if not per_eng[e]:
                continue
            seq = per_eng[e]

            def section(engobj, seq=seq, e=e):
                for (fn, waits, is_dma, tok) in seq:
                    tgt = {}
                    for w in waits:
                        sem, val = self.resolved[w]
                        prev = tgt.get(id(sem))
                        tgt[id(sem)] = (sem, val if prev is None else max(val, prev[1]))
                    for sem, val in tgt.values():
                        engobj.wait_ge(sem, val)
                    if is_dma:
                        nd = self.dma_count[e]
                        lane = nd % self.NLANE
                        if nd >= self.NLANE:
                            engobj.wait_ge(self.dma_sem[e][lane],
                                           16 * (nd // self.NLANE))
                        inst = fn(engobj)
                        self.dma_count[e] += 1
                        inst.then_inc(self.dma_sem[e][lane], 16)
                        continue
                    inst = fn(engobj)
                    if False:
                        pass
                    elif tok in self.awaited:
                        self.comp_emitted[e] += 1
                        inst.then_inc(self.comp_sem[e], 1)

            getattr(self.blk, e)(section)
        for e in ENGINES:
            self.pending_base[e] += self.n_ops[e]
            self.n_ops[e] = 0
        self.ops = []

    def drain_dmas(self):
        self.flush()
        for e in ("sync", "scalar"):
            cnt = self.dma_count[e]
            if cnt == 0:
                continue
            sems = self.dma_sem[e]

            def section(engobj, cnt=cnt, sems=sems):
                for lane in range(self.NLANE):
                    n = (cnt - lane + self.NLANE - 1) // self.NLANE
                    if n > 0:
                        engobj.wait_ge(sems[lane], 16 * n)

            getattr(self.blk, e)(section)

    def join(self):
        toks = [t for t in self.last_tok.values() if t is not None]
        newt = {}
        for e in ENGINES:
            newt[e] = self.do(e, lambda x: x.nop(), wait=toks)
        self.flush()
        return newt


def build_program(debug=(), upto='p6', races=True):
    nc = bass.Bass("TRN2", target_bir_lowering=False,
                   detect_race_conditions=races)
    es = ExitStack()
    try:
        _build_body(nc, es, debug, upto)
    except _Stop:
        pass
    es.close()
    return nc


def _build_body(nc, es, debug, upto):

    def dram(name, shape, dt):
        if name in debug:
            return nc.dram_tensor(name, shape, dt, kind="ExternalOutput")
        return nc.dram_tensor(name, shape, dt)

    d_xT = nc.dram_tensor("xT", [D, S], F32, kind="ExternalInput")
    d_wq = nc.dram_tensor("wq_p", [H, 128, 4096], F32, kind="ExternalInput")
    d_wo = nc.dram_tensor("wo_p", [32, 128, 4096], F32, kind="ExternalInput")
    d_wkvd = nc.dram_tensor("wkvd", [D, 512], F32, kind="ExternalInput")
    d_wkup = nc.dram_tensor("wkup4", [4, 128, 1024], F32, kind="ExternalInput")
    d_wvup = nc.dram_tensor("wvup4", [4, 128, 1024], F32, kind="ExternalInput")
    d_qwgu = nc.dram_tensor("qwgu_p", [GU_M, 128, 512], I32, kind="ExternalInput")
    d_qzgu = nc.dram_tensor("qzgu", [32, 2752], I32, kind="ExternalInput")
    d_sgu = nc.dram_tensor("sgu", [32, 22016], F32, kind="ExternalInput")
    d_qwdn = nc.dram_tensor("qwdn_p", [32, 128, 1376], I32, kind="ExternalInput")
    d_qzdn = nc.dram_tensor("qzdn", [86, 512], I32, kind="ExternalInput")
    d_sdn = nc.dram_tensor("sdn", [86, 4096], F32, kind="ExternalInput")
    d_indgu = nc.dram_tensor("indgu", [128, 32 * 32], F32, kind="ExternalInput")
    d_inddn = nc.dram_tensor("inddn", [128, 86 * 86], F32, kind="ExternalInput")
    d_mtri = nc.dram_tensor("mtri", [128, 128], F32, kind="ExternalInput")

    d_xnT = dram("xnT_d", [32, 128, S], BF16)
    d_qT = dram("qT_d", [H, 128, S], BF16)
    d_kT = dram("kT_d", [HK, 128, S], BF16)
    d_v = dram("v_d", [16, 128, HK * HD], BF16)
    d_negc = dram("negc_d", [H, S], BF16)
    d_sgur = dram("sgur_d", [GU_M, 4096], BF16)
    d_zsgu = dram("zsgu_d", [32, 22016], BF16)
    d_sdnr = dram("sdnr_d", [32, 11008], BF16)
    d_zsdn = dram("zsdn_d", [86, 4096], BF16)
    d_hT = dram("hT_d", [DN_K, 128, S], BF16)
    d_w1dn = dram("w1dn_d", [32, 128, 11008], BF16)
    d_gT = dram("gT_d", [32, 128, S], BF16)
    d_oT = dram("oT_d", [H, 128, S], BF16)
    d_out = nc.dram_tensor("outT", [D, S], F32, kind="ExternalOutput")

    sbuf = lambda name, shape, dt: es.enter_context(nc.sbuf_tensor(name, shape, dt))
    P = Prog(nc, es)

    ones1 = sbuf("ones1", [1, 128], BF16)
    ones1f = sbuf("ones1f", [1, 128], F32)
    ones128 = sbuf("ones128", [128, 1], BF16)
    sqkn = sbuf("sqkn", [1, 8], F32)
    xsumn = sbuf("xsumn", [32, S], BF16)
    row1 = sbuf("row1", [1, 512], F32)
    row2 = sbuf("row2", [1, 512], F32)
    c0row = sbuf("c0row", [1, 32], F32)

    P.do("vector", lambda v: v.memset(ones1[:], 1.0))
    P.do("vector", lambda v: v.memset(ones1f[:], 1.0))
    P.do("vector", lambda v: v.memset(ones128[:], 1.0))

    with ExitStack() as ph:
        pb = lambda n, sh, dt: ph.enter_context(nc.sbuf_tensor(n, sh, dt))
        sf = pb("p0_sf", [86, 5504], F32)
        sbf = pb("p0_sb", [86, 5504], BF16)
        qzi = pb("p0_qz", [86, 688], I32)
        zi = pb("p0_zi", [86, 5504], I32)
        zbf = pb("p0_zb", [86, 5504], BF16)
        zsb = pb("p0_zs", [86, 5504], BF16)

        vtok = None
        for ch in range(4):
            io0 = ch * 5504
            w1 = P.do("sync", lambda sy, io0=io0: sy.dma_start(
                sf[0:32, :], d_sgu[:, io0:io0 + 5504]), wait=(vtok,), dma=True)
            w2 = P.do("sync", lambda sy, io0=io0: sy.dma_start(
                qzi[0:32, :], d_qzgu[:, io0 // 8: io0 // 8 + 688]), dma=True)
            P.do("vector", lambda v: v.tensor_copy(sbf[0:32, :], sf[0:32, :]),
                 wait=(w1, vtok))
            for idx in range(8):
                P.do("vector", lambda v, idx=idx: v.tensor_scalar(
                    bass.AP(zi, idx, [[5504, 32], [8, 688]]), qzi[0:32, :],
                    4 * idx, 15, OP.logical_shift_right, OP.bitwise_and), wait=(w2,))
            P.do("vector", lambda v: v.tensor_copy(zbf[0:32, :], zi[0:32, :]))
            vt = P.do("vector", lambda v: v.tensor_tensor(
                zsb[0:32, :], zbf[0:32, :], sbf[0:32, :], OP.mult))
            mt0 = io0 // 128
            P.do("scalar", lambda sc, mt0=mt0: sc.dma_start(
                d_sgur[:, :].rearrange("m (g l) -> g m l", g=32)[:, mt0:mt0 + 43, :],
                sbf[0:32, :].rearrange("g (m l) -> g m l", m=43)),
                wait=(vt,), dma=True)
            vtok = P.do("scalar", lambda sc, io0=io0: sc.dma_start(
                d_zsgu[:, io0:io0 + 5504], zsb[0:32, :]), wait=(vt,), dma=True)

        w1 = P.do("sync", lambda sy: sy.dma_start(sf[:, 0:4096], d_sdn[:]),
                  wait=(vtok,), dma=True)
        w2 = P.do("sync", lambda sy: sy.dma_start(qzi[:, 0:512], d_qzdn[:]), dma=True)
        P.do("vector", lambda v: v.tensor_copy(sbf[:, 0:4096], sf[:, 0:4096]),
             wait=(w1, vtok))
        for idx in range(8):
            P.do("vector", lambda v, idx=idx: v.tensor_scalar(
                bass.AP(zi, idx, [[5504, 86], [8, 512]]), qzi[:, 0:512],
                4 * idx, 15, OP.logical_shift_right, OP.bitwise_and), wait=(w2,))
        P.do("vector", lambda v: v.tensor_copy(zbf[:, 0:4096], zi[:, 0:4096]))
        vt = P.do("vector", lambda v: v.tensor_tensor(
            zsb[:, 0:4096], zbf[:, 0:4096], sbf[:, 0:4096], OP.mult))
        P.do("scalar", lambda sc: sc.dma_start(
            d_sdnr[:, :].rearrange("m (g l) -> g m l", g=86),
            sbf[:, 0:4096].rearrange("g (m l) -> g m l", m=32)), wait=(vt,), dma=True)
        P.do("scalar", lambda sc: sc.dma_start(d_zsdn[:], zsb[:, 0:4096]),
             wait=(vt,), dma=True)
        P.join()

    if upto == 'p0':
        P.drain_dmas()
        raise _Stop

    with ExitStack() as ph:
        pb = lambda n, sh, dt: ph.enter_context(nc.sbuf_tensor(n, sh, dt))
        psm = lambda n, sh: ph.enter_context(nc.psum_tensor(n, sh, F32))
        ct = pb("ct_s", [128, 4 * S], BF16)
        ln_stack = ExitStack()
        pbl = lambda n, sh, dt: ln_stack.enter_context(nc.sbuf_tensor(n, sh, dt))
        psl = lambda n, sh: ln_stack.enter_context(nc.psum_tensor(n, sh, F32))
        indgu = pbl("indgu_s", [128, 32 * 32], BF16)
        cstf = pbl("p1_cst", [128, 1024], F32)
        xf = pbl("p1_xf", [128, 32 * 512], F32)
        xbf = pbl("p1_xbf", [128, 2 * 512], BF16)
        xsq = pbl("p1_xsq", [128, 2 * 512], BF16)
        xnb = pbl("p1_xnb", [128, 32 * 512], BF16)
        tmpf = pbl("p1_tmp", [128, 512], F32)
        wkvf = pbl("p1_wkvf", [128, 2 * 512], F32)
        wkv = pbl("p1_wkv", [128, 32 * 512], BF16)
        row3 = pbl("p1_row3", [1, 512], F32)
        row4 = pbl("p1_row4", [1, 512], F32)
        p_cat = psl("p1_cat", [128, 4 * 512])
        p_xs = psl("p1_xs", [32, 512])
        p_bc1 = psl("p1_bc1", [128, 512])
        p_bc2 = psl("p1_bc2", [128, 512])
        p_st1 = p_cat[0:1, 0:512]
        p_st2 = p_cat[0:1, 512:1024]

        w = P.do("sync", lambda sy: sy.dma_start(cstf[:], d_indgu[:]), dma=True)
        ind_ok = P.do("vector", lambda v: v.tensor_copy(indgu[:], cstf[:]), wait=(w,))

        ctoks = {}
        for j in range(32):
            w = P.do("sync", lambda sy, j=j: sy.dma_start(
                wkvf[:, (j % 2) * 512:(j % 2 + 1) * 512],
                d_wkvd[j * 128:(j + 1) * 128, :]),
                wait=(ctoks.get(j - 2),), dma=True)
            ctoks[j] = P.do("scalar", lambda sc, j=j: sc.activation(
                wkv[:, j * 512:(j + 1) * 512],
                wkvf[:, (j % 2) * 512:(j % 2 + 1) * 512], AF.Copy), wait=(w,))
        wkv_done = ctoks[31]

        cat_evac = None
        xn_used = None
        for c in range(4):
            tc = c * 512
            xw = []
            for j in range(32):
                xw.append(P.do("sync", lambda sy, j=j, tc=tc: sy.dma_start(
                    xf[:, j * 512:(j + 1) * 512],
                    d_xT[j * 128:(j + 1) * 128, tc:tc + 512]),
                    wait=(xn_used if j == 0 else None,), dma=True))
            mm1 = mm2 = None
            for j in range(32):
                b = (j % 2) * 512
                vt1 = P.do("vector", lambda v, j=j, b=b: v.tensor_copy(
                    xbf[:, b:b + 512], xf[:, j * 512:(j + 1) * 512]),
                    wait=(xw[j], mm1 if j >= 2 else None,
                          cat_evac if j <= 1 else None))
                vt2 = P.do("vector", lambda v, j=j, b=b: v.tensor_tensor(
                    xsq[:, b:b + 512], xbf[:, b:b + 512], xbf[:, b:b + 512], OP.mult))
                mm1 = P.do("tensor", lambda t, j=j, b=b: t.matmul(
                    p_st1, ones128[:], xbf[:, b:b + 512],
                    start=(j == 0), stop=(j == 31)), wait=(vt1,))
                mm2 = P.do("tensor", lambda t, j=j, b=b: t.matmul(
                    p_st2, ones128[:], xsq[:, b:b + 512],
                    start=(j == 0), stop=(j == 31)), wait=(vt2,))
            rv = P.do("vector", lambda v: v.tensor_scalar(
                row1[:], p_st1, 1.0 / D, None, OP.mult), wait=(mm1, mm2))
            P.do("vector", lambda v: v.tensor_scalar(
                row2[:], p_st2, 1.0 / D, None, OP.mult))
            P.do("vector", lambda v: v.tensor_tensor(row3[:], row1[:], row1[:], OP.mult))
            P.do("vector", lambda v: v.tensor_tensor(row2[:], row2[:], row3[:], OP.subtract))
            P.do("vector", lambda v: v.tensor_scalar(row2[:], row2[:], 1e-5, None, OP.add))
            rv = P.do("vector", lambda v: v.reciprocal(row3[:], row2[:]))
            rs = P.do("scalar", lambda sc: sc.activation(row4[:], row3[:], AF.Sqrt),
                      wait=(rv,))
            rv = P.do("vector", lambda v: v.tensor_tensor(
                row2[:], row1[:], row4[:], OP.mult), wait=(rs,))
            bc = P.do("tensor", lambda t: t.matmul(
                p_bc1[:], ones1f[:], row4[:], start=True, stop=True), wait=(rs, rv))
            bc = P.do("tensor", lambda t: t.matmul(
                p_bc2[:], ones1f[:], row2[:], start=True, stop=True), wait=(rv,))
            vx = None
            for j in range(32):
                sl = slice(j * 512, (j + 1) * 512)
                P.do("vector", lambda v, sl=sl: v.tensor_tensor(
                    tmpf[:], xf[:, sl], p_bc1[:], OP.mult), wait=(bc,))
                vx = P.do("vector", lambda v, sl=sl: v.tensor_tensor(
                    xnb[:, sl], tmpf[:], p_bc2[:], OP.subtract))
            mmc = mmx = None
            for j in range(32):
                sl = slice(j * 512, (j + 1) * 512)
                for mt in range(4):
                    mmc = P.do("tensor", lambda t, j=j, mt=mt, sl=sl: t.matmul(
                        p_cat[:, mt * 512:(mt + 1) * 512],
                        wkv[:, j * 512 + mt * 128: j * 512 + (mt + 1) * 128],
                        xnb[:, sl], start=(j == 0), stop=(j == 31)),
                        wait=(vx if j == 0 and mt == 0 else None,
                              wkv_done if j == 0 and mt == 0 else None,
                              cat_evac if j == 0 and mt == 0 else None))
                mmx = P.do("tensor", lambda t, j=j, sl=sl: t.matmul(
                    p_xs[:], indgu[:, j * 32:(j + 1) * 32], xnb[:, sl],
                    start=(j == 0), stop=(j == 31)),
                    wait=(ind_ok if j == 0 and c == 0 else None,))
            for mt in range(4):
                cat_evac = P.do("scalar", lambda sc, mt=mt, tc=tc: sc.activation(
                    ct[:, mt * S + tc: mt * S + tc + 512],
                    p_cat[:, mt * 512:(mt + 1) * 512], AF.Copy), wait=(mmc,))
            cat_evac = P.do("scalar", lambda sc, tc=tc: sc.activation(
                xsumn[:, tc:tc + 512], p_xs[:], AF.Copy), wait=(mmx,))
            xn_used = None
            for j in range(32):
                xn_used = P.do("scalar", lambda sc, j=j, tc=tc: sc.dma_start(
                    d_xnT[j, :, tc:tc + 512], xnb[:, j * 512:(j + 1) * 512]),
                    wait=(vx, mmc), dma=True)
        P.join()
        ln_stack.close()

        if upto != 'p1a':
            with ExitStack() as ph2:
                pb2 = lambda n, sh, dt: ph2.enter_context(nc.sbuf_tensor(n, sh, dt))
                psm2 = lambda n, sh: ph2.enter_context(nc.psum_tensor(n, sh, F32))
                wupf = pb2("p1_wupf", [128, 2 * 1024], F32)
                wku = pb2("p1_wku", [128, 4 * 1024], BF16)
                wvu = pb2("p1_wvu", [128, 4 * 1024], BF16)
                kbuf = pb2("p1_kbuf", [128, 2 * 2048], BF16)
                k2b = pb2("p1_k2", [128, 512], BF16)
                vbuf = pb2("p1_vbuf", [128, 2 * 512], BF16)
                p_kv = psm2("p1_pkv", [128, 4 * 512])
                p_kn = psm2("p1_pkn", [1, 2048])

                toks = {}
                i = 0
                for (dsrc, dst) in [(d_wkup, wku), (d_wvup, wvu)]:
                    for j0 in range(4):
                        b = (i % 2) * 1024
                        w = P.do("sync", lambda sy, dsrc=dsrc, j0=j0, b=b: sy.dma_start(
                            wupf[:, b:b + 1024], dsrc[j0]),
                            wait=(toks.get(i - 2),), dma=True)
                        toks[i] = P.do("scalar", lambda sc, dst=dst, j0=j0, b=b: sc.activation(
                            dst[:, j0 * 1024:(j0 + 1) * 1024], wupf[:, b:b + 1024], AF.Copy),
                            wait=(w,))
                        i += 1
                wup_done = toks[7]

                kev = {}
                kspill = {}
                kst_done = {}
                kn_done = None
                k2_used = None
                for h2 in range(HK):
                    ktok = None
                    for hb in range(8):
                        idx = h2 * 8 + hb
                        bk = idx % 4
                        Bp = 64 * (hb % 2)
                        mm = None
                        for j0 in range(4):
                            mm = P.do("tensor", lambda t, h2=h2, hb=hb, Bp=Bp, j0=j0, bk=bk: t.matmul(
                                p_kv[:, bk * 512: bk * 512 + 256],
                                wku[Bp:Bp + 64, j0 * 1024 + h2 * 128: j0 * 1024 + (h2 + 1) * 128],
                                bass.AP(ct, Bp * (4 * S) + (hb // 2) * S + j0,
                                        [[4 * S, 64], [8, 256]]),
                                start=(j0 == 0), stop=(j0 == 3)),
                                wait=(wup_done if idx == 0 and j0 == 0 else None,
                                      kev.get(idx - 4) if j0 == 0 else None))
                        ev = P.do("scalar", lambda sc, h2=h2, hb=hb, bk=bk: sc.activation(
                            kbuf[:, (h2 % 2) * 2048 + hb * 256:(h2 % 2) * 2048 + (hb + 1) * 256],
                            p_kv[:, bk * 512: bk * 512 + 256], AF.Copy),
                            wait=(mm, kspill.get(h2 - 2) if hb == 0 else None,
                                  kst_done.get(h2 - 2) if hb == 0 else None))
                        kev[idx] = ev
                    qn = None
                    for chk in range(4):
                        v2 = P.do("vector", lambda v, h2=h2, chk=chk: v.tensor_tensor(
                            k2b[:],
                            kbuf[:, (h2 % 2) * 2048 + chk * 512:(h2 % 2) * 2048 + (chk + 1) * 512],
                            kbuf[:, (h2 % 2) * 2048 + chk * 512:(h2 % 2) * 2048 + (chk + 1) * 512],
                            OP.mult),
                            wait=(ev, k2_used if chk == 0 else None))
                        qn = P.do("tensor", lambda t, chk=chk: t.matmul(
                            p_kn[0:1, chk * 512:(chk + 1) * 512], ones128[:], k2b[:],
                            start=True, stop=True),
                            wait=(v2, kn_done if chk == 0 else None))
                        k2_used = qn
                    kst_done[h2] = qn
                    kn_done = P.do("vector", lambda v, h2=h2: v.reduce_max(
                        row1[0:1, h2:h2 + 1], p_kn[0:1, :], mybir.AxisListType.X),
                        wait=(qn,))
                    kspill[h2] = P.do("scalar", lambda sc, h2=h2: sc.dma_start(
                        d_kT[h2], kbuf[:, (h2 % 2) * 2048:(h2 % 2 + 1) * 2048]),
                        wait=(ev,), dma=True)
                sq = P.do("scalar", lambda sc: sc.activation(
                    row2[0:1, 0:8], row1[0:1, 0:8], AF.Sqrt), wait=(kn_done,))
                sqk_done = P.do("vector", lambda v: v.tensor_scalar(
                    sqkn[:], row2[0:1, 0:8], -SAFETY, None, OP.mult), wait=(sq,))

                if upto != 'p1k':
                    vev = {}
                    vspill = {}
                    for it in range(16):
                        hb = it // 2
                        Bp = 64 * (hb % 2)
                        toff = 1024 * (it % 2)
                        for nch in range(2):
                            idx = it * 2 + nch
                            pk = p_kv[:, (idx % 2) * 512:(idx % 2 + 1) * 512]
                            mm = None
                            for j0 in range(4):
                                mm = P.do("tensor", lambda t, hb=hb, Bp=Bp, toff=toff, j0=j0, nch=nch, pk=pk: t.matmul(
                                    pk,
                                    bass.AP(ct, Bp * (4 * S) + (hb // 2) * S + toff + j0 + 4,
                                            [[4 * S, 64], [8, 128]]),
                                    wvu[Bp:Bp + 64, j0 * 1024 + nch * 512: j0 * 1024 + (nch + 1) * 512],
                                    start=(j0 == 0), stop=(j0 == 3)),
                                    wait=(vev.get(idx - 2) if j0 == 0 else None,))
                            ev = P.do("scalar", lambda sc, idx=idx, pk=pk: sc.activation(
                                vbuf[:, (idx % 2) * 512:(idx % 2 + 1) * 512], pk, AF.Copy),
                                wait=(mm, vspill.get(idx - 2)))
                            vev[idx] = ev
                            vspill[idx] = P.do("scalar", lambda sc, it=it, nch=nch, idx=idx: sc.dma_start(
                                d_v[it, :, nch * 512:(nch + 1) * 512],
                                vbuf[:, (idx % 2) * 512:(idx % 2 + 1) * 512]),
                                wait=(ev,), dma=True)
                    P.join()

    if upto in ('p1', 'p1a', 'p1k'):
        P.drain_dmas()
        raise _Stop

    with ExitStack() as ph:
        pb = lambda n, sh, dt: ph.enter_context(nc.sbuf_tensor(n, sh, dt))
        xnr = pb("xnr", [128, 32 * S], BF16)

        xw = []
        for j in range(32):
            xw.append(P.do("sync", lambda sy, j=j: sy.dma_start(
                xnr[:, j * S:(j + 1) * S], d_xnT[j]), dma=True))
        xnr_ready = xw[-1]

        with ExitStack() as p2:
            pb2 = lambda n, sh, dt: p2.enter_context(nc.sbuf_tensor(n, sh, dt))
            psm2 = lambda n, sh: p2.enter_context(nc.psum_tensor(n, sh, F32))
            wqs = pb2("p2_wqs", [128, 2048], F32)
            wqb = pb2("p2_wqb", [128, 2 * 4096], BF16)
            qb = pb2("p2_qb", [128, 2 * S], BF16)
            q2t = pb2("p2_q2", [128, 2 * 512], BF16)
            rowq = pb2("p2_rowq", [1, S], F32)
            p_q = psm2("p2_pq", [128, 4 * 512])
            p_qn = psm2("p2_pqn", [1, 2048])

            cast_done = {}
            ev_done = {}
            q_spill = {}
            n_spill = {}
            qn_last = {}
            vn_last = {}
            for m in range(H + 1):
                if m < H:
                    for half in range(2):
                        w = P.do("sync", lambda sy, m=m, half=half: sy.dma_start(
                            wqs[:], d_wq[m, :, half * 2048:(half + 1) * 2048]),
                            wait=(cast_done.get((m, half - 1)) or cast_done.get((m - 1, 1)),
                                  ev_done.get(m - 2) if half == 0 else None), dma=True)
                        cast_done[(m, half)] = P.do("scalar", lambda sc, m=m, half=half: sc.activation(
                            wqb[:, (m % 2) * 4096 + half * 2048:(m % 2) * 4096 + (half + 1) * 2048],
                            wqs[:], AF.Copy), wait=(w,))
                if m > 0:
                    mp = m - 1
                    mm = None
                    for j in range(32):
                        for nch in range(4):
                            mm = P.do("tensor", lambda t, mp=mp, j=j, nch=nch: t.matmul(
                                p_q[:, nch * 512:(nch + 1) * 512],
                                wqb[:, (mp % 2) * 4096 + j * 128:(mp % 2) * 4096 + (j + 1) * 128],
                                xnr[:, j * S + nch * 512: j * S + (nch + 1) * 512],
                                start=(j == 0), stop=(j == 31)),
                                wait=(cast_done[(mp, 1)] if j == 0 and nch == 0 else None,
                                      xnr_ready if j == 0 and nch == 0 else None,
                                      ev_done.get(mp - 1) if j == 0 and nch == 0 else None))
                    ev = P.do("scalar", lambda sc, mp=mp: sc.activation(
                        qb[:, (mp % 2) * S:(mp % 2 + 1) * S], p_q[:], AF.Copy, scale=SCALE),
                        wait=(mm, q_spill.get(mp - 2), qn_last.get(mp - 2)))
                    ev_done[mp] = ev
                    q_spill[mp] = P.do("scalar", lambda sc, mp=mp: sc.dma_start(
                        d_qT[mp], qb[:, (mp % 2) * S:(mp % 2 + 1) * S]),
                        wait=(ev,), dma=True)
                    qn = None
                    for nch in range(4):
                        bq = (nch % 2) * 512
                        v2 = P.do("vector", lambda v, mp=mp, nch=nch, bq=bq: v.tensor_tensor(
                            q2t[:, bq:bq + 512],
                            qb[:, (mp % 2) * S + nch * 512:(mp % 2) * S + (nch + 1) * 512],
                            qb[:, (mp % 2) * S + nch * 512:(mp % 2) * S + (nch + 1) * 512],
                            OP.mult),
                            wait=(ev if nch == 0 else None,
                                  qn_last.get(mp - 1) if nch == 0 else None,
                                  qn if nch >= 2 else None))
                        qn = P.do("tensor", lambda t, nch=nch, bq=bq: t.matmul(
                            p_qn[0:1, nch * 512:(nch + 1) * 512], ones128[:],
                            q2t[:, bq:bq + 512], start=True, stop=True),
                            wait=(v2, (n_spill.get(mp - 1) or sqk_done) if nch == 0 else None))
                    qn_last[mp] = qn
                    sq = P.do("scalar", lambda sc: sc.activation(
                        rowq[:], p_qn[0:1, :], AF.Sqrt),
                        wait=(qn, vn_last.get(mp - 1), sqk_done if mp == 0 else None))
                    vn = P.do("vector", lambda v: v.reduce_max(
                        row2[0:1, 9:10], rowq[:], mybir.AxisListType.X), wait=(sq,))
                    vn = P.do("vector", lambda v, mp=mp: v.tensor_scalar(
                        c0row[0:1, mp:mp + 1], row2[0:1, 9:10],
                        sqkn[0:1, mp // 4:mp // 4 + 1], None, OP.mult), wait=(vn,))
                    vn_last[mp] = vn
                    n_spill[mp] = vn
            jt = P.join()
            global _c0_done_tok
            _c0_done_tok = jt.get("vector")

        with ExitStack() as p3:
            pb3 = lambda n, sh, dt: p3.enter_context(nc.sbuf_tensor(n, sh, dt))
            psm3 = lambda n, sh: p3.enter_context(nc.psum_tensor(n, sh, F32))
            qwb = pb3("p3_qw", [128, 512], I32)
            nib = pb3("p3_nib", [128, 2048], I32)
            w1g = pb3("p3_w1g", [128, 2 * 4096], BF16)
            w1u = pb3("p3_w1u", [128, 2 * 4096], BF16)
            srow = pb3("p3_srow", [1, 4096], BF16)
            zsl = pb3("p3_zsl", [32, 4 * 128], BF16)
            hg = pb3("p3_hg", [128, S], BF16)
            hu = pb3("p3_hu", [128, S], BF16)
            hbb = pb3("p3_hb", [128, S], BF16)
            p_acc = psm3("p3_acc", [128, 4 * 512])
            p_bc = psm3("p3_bc", [128, 4 * 512])

            NP = DN_K
            bank_last = [None] * 4
            deq_state = {}
            deq_done = {}
            acc_tok = {}
            unp_done = {}
            bc_done = {}
            ev_done = {}
            hmul_done = {}
            hdma = {}

            def deq_io(pair, gu):
                m = pair + NP * gu
                wq = P.do("sync", lambda sy, m=m: sy.dma_start(qwb[:], d_qwgu[m]),
                          wait=(unp_done.get((pair, gu - 1) if gu == 1 else (pair - 1, 1)),), dma=True)
                ws = P.do("sync", lambda sy, m=m: sy.dma_start(srow[:], d_sgur[m:m + 1, :]),
                          wait=(bc_done.get((pair, gu - 1) if gu == 1 else (pair - 1, 1)),), dma=True)
                zslot = (2 * pair + gu) % 4
                wz = P.do("sync", lambda sy, m=m, zslot=zslot: sy.dma_start(
                    zsl[:, zslot * 128:(zslot + 1) * 128], d_zsgu[:, m * 128:(m + 1) * 128]),
                    wait=(acc_tok.get((pair - 2, gu)),), dma=True)
                deq_state[(pair, gu)] = {"wq": wq, "ws": ws, "wz": wz, "tts": []}

            def deq_unpack(pair, gu, half):
                st = deq_state[(pair, gu)]
                un = None
                for idx in range(8):
                    un = P.do("vector", lambda v, idx=idx, half=half: v.tensor_scalar(
                        bass.AP(nib, idx, [[2048, 128], [8, 256]]),
                        qwb[:, half * 256:(half + 1) * 256],
                        4 * idx, 15, OP.logical_shift_right, OP.bitwise_and),
                        wait=(st["wq"] if idx == 0 and half == 0 else None,))
                if half == 1:
                    unp_done[(pair, gu)] = un

            def deq_chunk(pair, gu, q):
                st = deq_state[(pair, gu)]
                w1 = w1g if gu == 0 else w1u
                wbase = (pair % 2) * 4096
                bk = q % 4
                bc = P.do("tensor", lambda t, q=q, bk=bk: t.matmul(
                    p_bc[:, bk * 512:(bk + 1) * 512], ones1[:],
                    srow[:, q * 512:(q + 1) * 512], start=True, stop=True),
                    wait=(st["ws"] if q == 0 else None, bank_last[bk]))
                tt = P.do("vector", lambda v, q=q, bk=bk, w1=w1, wbase=wbase: v.tensor_tensor(
                    w1[:, wbase + q * 512: wbase + (q + 1) * 512],
                    nib[:, (q % 4) * 512:(q % 4 + 1) * 512],
                    p_bc[:, bk * 512:(bk + 1) * 512], OP.mult),
                    wait=(bc, acc_tok.get((pair - 2, gu)) if q == 0 else None))
                bank_last[bk] = tt
                st["tts"].append(tt)
                if q == 7:
                    bc_done[(pair, gu)] = tt
                    deq_done[(pair, gu)] = (tt, st["wz"])

            def acc(pair, gu, nxt):
                zslot = (2 * pair + gu) % 4
                w1 = w1g if gu == 0 else w1u
                wbase = (pair % 2) * 4096
                dq, wz = deq_done[(pair, gu)]
                if nxt is not None:
                    deq_io(*nxt)
                    deq_unpack(*nxt, 0)
                mm = None
                k = 0
                for j in range(32):
                    for nch in range(4):
                        mm = P.do("tensor", lambda t, j=j, nch=nch, w1=w1, wbase=wbase: t.matmul(
                            p_acc[:, nch * 512:(nch + 1) * 512],
                            w1[:, wbase + j * 128: wbase + (j + 1) * 128],
                            xnr[:, j * S + nch * 512: j * S + (nch + 1) * 512],
                            start=(j == 0), stop=False),
                            wait=(dq if j == 0 and nch == 0 else None,
                                  ev_done.get((pair, gu - 1) if gu == 1 else (pair - 1, 1)) if j == 0 and nch == 0 else None))
                        k += 1
                        if nxt is not None and k % 14 == 0:
                            q = k // 14 - 1
                            if q < 8:
                                if q == 4:
                                    deq_unpack(*nxt, 1)
                                deq_chunk(*nxt, q)
                for nch in range(4):
                    mm = P.do("tensor", lambda t, nch=nch, zslot=zslot: t.matmul(
                        p_acc[:, nch * 512:(nch + 1) * 512],
                        zsl[:, zslot * 128:(zslot + 1) * 128],
                        xsumn[:, nch * 512:(nch + 1) * 512], start=False, stop=(nch == 3)),
                        wait=(wz if nch == 0 else None,))
                acc_tok[(pair, gu)] = mm
                dst = hg if gu == 0 else hu
                fn = AF.Silu if gu == 0 else AF.Copy
                ev_done[(pair, gu)] = P.do("scalar", lambda sc, dst=dst, fn=fn: sc.activation(
                    dst[:], p_acc[:], fn),
                    wait=(mm, hmul_done.get(pair - 1) if gu == 0 else None))
                if gu == 1:
                    hm = P.do("vector", lambda v: v.tensor_tensor(
                        hbb[:], hg[:], hu[:], OP.mult),
                        wait=(ev_done[(pair, 0)], ev_done[(pair, 1)], hdma.get(pair - 1)))
                    hmul_done[pair] = hm
                    hdma[pair] = P.do("scalar", lambda sc, pair=pair: sc.dma_start(
                        d_hT[pair], hbb[:]), wait=(hm,), dma=True)

            for g0 in range(2):
                deq_io(0, g0)
                deq_unpack(0, g0, 0)
                for q in range(8):
                    if q == 4:
                        deq_unpack(0, g0, 1)
                    deq_chunk(0, g0, q)
            for pair in range(NP):
                acc(pair, 0, (pair + 1, 0) if pair + 1 < NP else None)
                acc(pair, 1, (pair + 1, 1) if pair + 1 < NP else None)
            P.join()

    if upto == 'p3':
        P.drain_dmas()
        raise _Stop

    with ExitStack() as ph:
        pb = lambda n, sh, dt: ph.enter_context(nc.sbuf_tensor(n, sh, dt))
        psm = lambda n, sh: ph.enter_context(nc.psum_tensor(n, sh, F32))
        qdb = pb("b_qd", [128, 2 * 1376], I32)
        nibd = pb("b_nib", [128, 11008], I32)
        nibb = pb("b_nibb", [128, 11008], BF16)
        w1d = pb("b_w1d", [128, 2 * 11008], BF16)
        sdr = pb("b_sdr", [1, 2 * 11008], BF16)
        p_bc = psm("b_bc", [128, 2 * 512])

        unp_prev = {}
        tt_prev = {}
        dma_prev = {}
        bank_last = [None, None]
        for m in range(32):
            b = (m % 2)
            wq = P.do("sync", lambda sy, m=m, b=b: sy.dma_start(
                qdb[:, b * 1376:(b + 1) * 1376], d_qwdn[m]),
                wait=(unp_prev.get(m - 2),), dma=True)
            ws = P.do("sync", lambda sy, m=m, b=b: sy.dma_start(
                sdr[:, b * 11008:(b + 1) * 11008], d_sdnr[m:m + 1, :]),
                wait=(tt_prev.get(m - 2),), dma=True)
            un = None
            for idx in range(8):
                un = P.do("vector", lambda v, idx=idx, b=b: v.tensor_scalar(
                    bass.AP(nibd, idx, [[11008, 128], [8, 1376]]),
                    qdb[:, b * 1376:(b + 1) * 1376],
                    4 * idx, 15, OP.logical_shift_right, OP.bitwise_and),
                    wait=(wq if idx == 0 else None,
                          tt_prev.get(m - 1) if idx == 0 else None))
            unp_prev[m] = un
            cst = P.do("vector", lambda v: v.tensor_copy(nibb[:], nibd[:]), wait=(un,))
            tt = None
            for q in range(22):
                n = min(512, 11008 - q * 512)
                bk = q % 2
                bc = P.do("tensor", lambda t, q=q, n=n, bk=bk, b=b: t.matmul(
                    p_bc[:, bk * 512:bk * 512 + n], ones1[:],
                    sdr[:, b * 11008 + q * 512: b * 11008 + q * 512 + n],
                    start=True, stop=True), wait=(ws if q == 0 else None, bank_last[bk]))
                tt = P.do("vector", lambda v, q=q, n=n, bk=bk, b=b: v.tensor_tensor(
                    w1d[:, b * 11008 + q * 512: b * 11008 + q * 512 + n],
                    nibb[:, q * 512:q * 512 + n],
                    p_bc[:, bk * 512:bk * 512 + n], OP.mult),
                    wait=(bc, cst if q == 0 else None,
                          dma_prev.get(m - 2) if q == 0 else None))
                bank_last[bk] = tt
            tt_prev[m] = tt
            dma_prev[m] = P.do("scalar", lambda sc, m=m, b=b: sc.dma_start(
                d_w1dn[m], w1d[:, b * 11008:(b + 1) * 11008]), wait=(tt,), dma=True)
        P.join()

    if upto == 'p3b':
        P.drain_dmas()
        raise _Stop

    with ExitStack() as ph:
        pb = lambda n, sh, dt: ph.enter_context(nc.sbuf_tensor(n, sh, dt))
        psm = lambda n, sh: ph.enter_context(nc.psum_tensor(n, sh, F32))
        inddn = pb("p4_ind", [128, 86 * 86], BF16)
        hc = pb("p4_hc", [128, 86 * 512], BF16)
        w1p = pb("p4_w1p", [128, 2 * 11008], BF16)
        zdn = pb("p4_zdn", [86, 4096], BF16)
        hsn = pb("p4_hsn", [86, 512], BF16)
        gb = pb("p4_gb", [128, 2 * 512], BF16)
        p_dn = psm("p4_dn", [128, 2 * 512])
        p_hs = psm("p4_hs", [86, 512])

        with ExitStack() as ph2:
            cstf = ph2.enter_context(nc.sbuf_tensor("p4_cst", [128, 86 * 86], F32))
            w = P.do("sync", lambda sy: sy.dma_start(cstf[:], d_inddn[:]), dma=True)
            ind_ok = P.do("vector", lambda v: v.tensor_copy(inddn[:], cstf[:]), wait=(w,))
            P.flush()
        wz = P.do("sync", lambda sy: sy.dma_start(zdn[:], d_zsdn[:]), dma=True)

        panel_used = {}
        g_spill = {}
        ev_tok = {}
        hs_prev = None
        for c in range(4):
            tc = c * 512
            hw = None
            for g in range(DN_K):
                hw = P.do("sync", lambda sy, g=g, tc=tc: sy.dma_start(
                    hc[:, g * 512:(g + 1) * 512], d_hT[g, :, tc:tc + 512]),
                    wait=(panel_used.get((c - 1, 31)) if g == 0 else None,), dma=True)
            hs = None
            for g in range(DN_K):
                hs = P.do("tensor", lambda t, g=g: t.matmul(
                    p_hs[:], inddn[:, g * 86:(g + 1) * 86], hc[:, g * 512:(g + 1) * 512],
                    start=(g == 0), stop=(g == DN_K - 1)),
                    wait=(hw if g == 0 else None,
                          ind_ok if g == 0 and c == 0 else None,
                          hs_prev if g == 0 else None))
            hs = P.do("scalar", lambda sc: sc.activation(hsn[:], p_hs[:], AF.Copy),
                      wait=(hs,))
            hs_prev = hs
            for m in range(32):
                b = m % 2
                wp = P.do("sync", lambda sy, m=m, b=b: sy.dma_start(
                    w1p[:, b * 11008:(b + 1) * 11008], d_w1dn[m]),
                    wait=(panel_used.get((c, m - 2), panel_used.get((c - 1, 30 + m)) if c > 0 else None),), dma=True)
                mm = None
                for g in range(DN_K):
                    mm = P.do("tensor", lambda t, g=g, b=b: t.matmul(
                        p_dn[:, b * 512:(b + 1) * 512],
                        w1p[:, b * 11008 + g * 128: b * 11008 + (g + 1) * 128],
                        hc[:, g * 512:(g + 1) * 512], start=(g == 0), stop=False),
                        wait=(wp if g == 0 else None, hs if g == 0 else None,
                              ev_tok.get((c, m - 2), ev_tok.get((c - 1, 30 + m)) if c > 0 else None) if g == 0 else None))
                mm = P.do("tensor", lambda t, m=m, b=b: t.matmul(
                    p_dn[:, b * 512:(b + 1) * 512], zdn[:, m * 128:(m + 1) * 128], hsn[:],
                    start=False, stop=True), wait=(wz if m == 0 and c == 0 else None,))
                panel_used[(c, m)] = mm
                ev = P.do("scalar", lambda sc, b=b: sc.activation(
                    gb[:, b * 512:(b + 1) * 512], p_dn[:, b * 512:(b + 1) * 512], AF.Silu),
                    wait=(mm, g_spill.get((c, m - 2), g_spill.get((c - 1, 30 + m)) if c > 0 else None)))
                ev_tok[(c, m)] = ev
                g_spill[(c, m)] = P.do("scalar", lambda sc, m=m, tc=tc, b=b: sc.dma_start(
                    d_gT[m, :, tc:tc + 512], gb[:, b * 512:(b + 1) * 512]),
                    wait=(ev,), dma=True)
        P.join()

    if upto == 'p4':
        P.drain_dmas()
        raise _Stop

    with ExitStack() as ph:
        pb = lambda n, sh, dt: ph.enter_context(nc.sbuf_tensor(n, sh, dt))
        psm = lambda n, sh: ph.enter_context(nc.psum_tensor(n, sh, F32))
        mtri = pb("p5_mtri", [128, 128], BF16)
        kTr = pb("p5_kT", [128, HK * S], BF16)
        vNr = pb("p5_vN", [128, 16 * HK * HD], BF16)
        qbr = pb("p5_qb", [128, 2 * S], BF16)
        negcb = pb("p5_ngb", [128, 2], F32)
        ptil = pb("p5_pt", [128, 3 * 512], BF16)
        ptmp = pb("p5_ptm", [128, 128], BF16)
        recs = pb("p5_rec", [128, 2 * 512], F32)
        otb = pb("p5_otb", [128, 2 * 512], BF16)
        p_st = psm("p5_st", [128, 2 * 512])
        p_cs = psm("p5_cs", [1, 2 * 512])
        p_ot = psm("p5_ot", [128, 2 * 512])
        p_rb = psm("p5_rb", [128, 2 * 512])

        with ExitStack() as ph2:
            cstf = ph2.enter_context(nc.sbuf_tensor("p5_cst", [128, 128], F32))
            w = P.do("sync", lambda sy: sy.dma_start(cstf[:], d_mtri[:]), dma=True)
            P.do("vector", lambda v: v.tensor_copy(mtri[:], cstf[:]), wait=(w,))
            P.flush()
        for h2 in range(HK):
            P.do("sync", lambda sy, h2=h2: sy.dma_start(
                kTr[:, h2 * S:(h2 + 1) * S], d_kT[h2]), dma=True)
        kv_ok = None
        for it in range(16):
            kv_ok = P.do("sync", lambda sy, it=it: sy.dma_start(
                vNr[:, it * 1024:(it + 1) * 1024], d_v[it]), dma=True)

        c0_done = _c0_done_tok
        q_used = {}
        st_last = [None, None]
        exp_done = [None, None, None]
        pt_last = [None, None, None]
        ptmp_last = None
        ot_used = [None, None]
        cs_used = [None, None]
        rb_used = [None, None]
        sp_used = [None, None]
        ng_evac = None
        rb_used_row = None
        for h in range(H):
            h2 = h // 4
            bh = (h % 2)
            wq = P.do("sync", lambda sy, h=h, bh=bh: sy.dma_start(
                qbr[:, bh * S:(bh + 1) * S], d_qT[h]), wait=(q_used.get(h - 2),), dma=True)
            nb = P.do("tensor", lambda t, h=h: t.matmul(
                p_rb[:, (h % 2) * 512:(h % 2) * 512 + 1], ones1f[:],
                c0row[0:1, h:h + 1], start=True, stop=True),
                wait=(rb_used[h % 2], c0_done if h == 0 else None))
            ng_evac = P.do("scalar", lambda sc, h=h: sc.activation(
                negcb[:, (h % 2):(h % 2) + 1], p_rb[:, (h % 2) * 512:(h % 2) * 512 + 1],
                AF.Copy), wait=(nb, ng_evac))
            rb_used[h % 2] = ng_evac
            for c in range(4):
                pp = (h * 4 + c) % 2
                ntile = 4 * c + 4
                pend = []
                cs = av = None
                for tt in range(ntile + 2):
                    if tt < ntile:
                        dj = tt - 4 * c
                        off = max(0, dj) * 128
                        qcol = bh * S + 512 * c
                        sb2 = tt % 2
                        sb3 = tt % 3
                        sc_mm = P.do("tensor", lambda t, sb2=sb2, qcol=qcol, h2=h2, tt=tt: t.matmul(
                            p_st[:, sb2 * 512:(sb2 + 1) * 512],
                            kTr[:, h2 * S + tt * 128: h2 * S + (tt + 1) * 128],
                            qbr[0:128, qcol:qcol + 512], start=True, stop=True),
                            wait=(wq if tt == 0 and c == 0 else None,
                                  kv_ok if h == 0 and c == 0 and tt == 0 else None,
                                  st_last[sb2]))
                    if (tt >= 2 and pend) or (tt >= ntile and pend):
                        psb2, ptt = pend.pop(0)
                        cs = P.do("tensor", lambda t, psb2=psb2, pp=pp, ptt=ptt, ntile=ntile: t.matmul(
                            p_cs[0:1, pp * 512:(pp + 1) * 512], ones128[:],
                            ptil[:, psb2 * 512:(psb2 + 1) * 512],
                            start=(ptt == 0), stop=(ptt == ntile - 1)),
                            wait=(exp_done[ptt % 3], cs_used[pp] if ptt == 0 else None))
                        av = P.do("tensor", lambda t, psb2=psb2, pp=pp, ptt=ptt, ntile=ntile, h2=h2: t.matmul(
                            p_ot[:, pp * 512:(pp + 1) * 512],
                            vNr[:, ptt * 1024 + h2 * 128: ptt * 1024 + (h2 + 1) * 128],
                            ptil[:, psb2 * 512:(psb2 + 1) * 512],
                            start=(ptt == 0), stop=(ptt == ntile - 1)),
                            wait=(ot_used[pp] if ptt == 0 else None,))
                        pt_last[psb2] = av
                    if tt >= ntile:
                        continue
                    if dj >= 0:
                        ex2 = None
                        if off + 128 < 512:
                            ex2 = P.do("scalar", lambda sc, sb2=sb2, sb3=sb3, off=off, h=h: sc.activation(
                                ptil[:, sb3 * 512 + off + 128:(sb3 + 1) * 512],
                                p_st[:, sb2 * 512 + off + 128:(sb2 + 1) * 512], AF.Exp,
                                bias=negcb[:, (h % 2):(h % 2) + 1]),
                                wait=(sc_mm, pt_last[sb3], ng_evac))
                        e2 = P.do("scalar", lambda sc, sb2=sb2, off=off, h=h: sc.activation(
                            ptmp[:], p_st[:, sb2 * 512 + off: sb2 * 512 + off + 128], AF.Exp,
                            bias=negcb[:, (h % 2):(h % 2) + 1]),
                            wait=(sc_mm, ptmp_last, ng_evac,
                                  pt_last[sb3] if ex2 is None else None))
                        ex = P.do("vector", lambda v, sb3=sb3, off=off: v.tensor_tensor(
                            ptil[:, sb3 * 512 + off: sb3 * 512 + off + 128], ptmp[:],
                            mtri[:], OP.mult), wait=(e2, ex2))
                        ptmp_last = ex
                        st_last[sb2] = e2
                        if off > 0:
                            ex = P.do("vector", lambda v, sb3=sb3, off=off: v.memset(
                                ptil[:, sb3 * 512: sb3 * 512 + off], 0.0),
                                wait=(pt_last[sb3],))
                        exp_done[sb3] = ex
                    else:
                        ex = P.do("scalar", lambda sc, sb2=sb2, sb3=sb3, h=h: sc.activation(
                            ptil[:, sb3 * 512:(sb3 + 1) * 512],
                            p_st[:, sb2 * 512:(sb2 + 1) * 512], AF.Exp,
                            bias=negcb[:, (h % 2):(h % 2) + 1]),
                            wait=(sc_mm, pt_last[sb3], ng_evac))
                        st_last[sb2] = ex
                        exp_done[sb3] = ex
                    pend.append((sb3, tt))
                rc = P.do("scalar", lambda sc, pp=pp: sc.activation(
                    row1[:], p_cs[0:1, pp * 512:(pp + 1) * 512], AF.Ln),
                    wait=(cs, rb_used_row,))
                cs_used[pp] = rc
                rb = P.do("tensor", lambda t, pp=pp: t.matmul(
                    p_rb[:, pp * 512:(pp + 1) * 512], ones1f[:], row1[:],
                    start=True, stop=True), wait=(rc, rb_used[pp]))
                rb_used_row = rb
                re = P.do("scalar", lambda sc, pp=pp: sc.activation(
                    recs[:, pp * 512:(pp + 1) * 512], p_rb[:, pp * 512:(pp + 1) * 512],
                    AF.Exp, scale=-1.0), wait=(rb,))
                rb_used[pp] = re
                ov = P.do("vector", lambda v, pp=pp: v.tensor_tensor(
                    otb[:, pp * 512:(pp + 1) * 512], p_ot[:, pp * 512:(pp + 1) * 512],
                    recs[:, pp * 512:(pp + 1) * 512], OP.mult),
                    wait=(re, av, sp_used[pp]))
                ot_used[pp] = ov
                sp = P.do("scalar", lambda sc, h=h, c=c, pp=pp: sc.dma_start(
                    d_oT[h, :, c * 512:(c + 1) * 512], otb[:, pp * 512:(pp + 1) * 512]),
                    wait=(ov,), dma=True)
                sp_used[pp] = sp
                if c == 3:
                    q_used[h] = sp
        P.join()

    if upto == 'p5':
        P.drain_dmas()
        raise _Stop

    with ExitStack() as ph:
        pb = lambda n, sh, dt: ph.enter_context(nc.sbuf_tensor(n, sh, dt))
        psm = lambda n, sh: ph.enter_context(nc.psum_tensor(n, sh, F32))
        otr = pb("p6_ot", [128, 32 * S], BF16)
        wos = pb("p6_wos", [128, 2048], F32)
        wob = pb("p6_wob", [128, 2 * 4096], BF16)
        gtb = pb("p6_gt", [128, 2 * S], BF16)
        xtb = pb("p6_xt", [128, S], F32)
        tm6 = pb("p6_tmp", [128, 512], F32)
        ob = pb("p6_ob", [128, S], F32)
        p_s = psm("p6_ps", [128, 2 * 2048])

        ow = []
        for j in range(32):
            ow.append(P.do("sync", lambda sy, j=j: sy.dma_start(
                otr[:, j * S:(j + 1) * S], d_oT[j]), dma=True))
        ot_ready = ow[-1]

        cast_done = {}
        g_done = {}
        x_done = {}
        acc_done = {}
        out_dma = {}
        vv_done = {}
        for m in range(33):
            if m < 32:
                for half in range(2):
                    w = P.do("sync", lambda sy, m=m, half=half: sy.dma_start(
                        wos[:], d_wo[m, :, half * 2048:(half + 1) * 2048]),
                        wait=(cast_done.get((m, half - 1)) or cast_done.get((m - 1, 1)),
                              acc_done.get(m - 2) if half == 0 else None), dma=True)
                    cast_done[(m, half)] = P.do("scalar", lambda sc, m=m, half=half: sc.activation(
                        wob[:, (m % 2) * 4096 + half * 2048:(m % 2) * 4096 + (half + 1) * 2048],
                        wos[:], AF.Copy), wait=(w,))
                g_done[m] = P.do("sync", lambda sy, m=m: sy.dma_start(
                    gtb[:, (m % 2) * S:(m % 2 + 1) * S], d_gT[m]),
                    wait=(vv_done.get(m - 2),), dma=True)
            if m > 0:
                mp = m - 1
                x_done[mp] = P.do("sync", lambda sy, mp=mp: sy.dma_start(
                    xtb[:], d_xT[mp * 128:(mp + 1) * 128, :]),
                    wait=(vv_done.get(mp - 1),), dma=True)
                mm = None
                for j in range(32):
                    for nch in range(4):
                        mm = P.do("tensor", lambda t, mp=mp, j=j, nch=nch: t.matmul(
                            p_s[:, (mp % 2) * 2048 + nch * 512:(mp % 2) * 2048 + (nch + 1) * 512],
                            wob[:, (mp % 2) * 4096 + j * 128:(mp % 2) * 4096 + (j + 1) * 128],
                            otr[:, j * S + nch * 512: j * S + (nch + 1) * 512],
                            start=(j == 0), stop=(j == 31)),
                            wait=(cast_done[(mp, 1)] if j == 0 and nch == 0 else None,
                                  ot_ready if j == 0 and nch == 0 else None,
                                  vv_done.get(mp - 2) if j == 0 and nch == 0 else None))
                acc_done[mp] = mm
                vv = None
                for nch in range(4):
                    sl = slice(nch * 512, (nch + 1) * 512)
                    P.do("vector", lambda v, mp=mp, nch=nch, sl=sl: v.tensor_tensor(
                        tm6[:], p_s[:, (mp % 2) * 2048 + nch * 512:(mp % 2) * 2048 + (nch + 1) * 512],
                        gtb[:, (mp % 2) * S + nch * 512:(mp % 2) * S + (nch + 1) * 512],
                        OP.mult),
                        wait=(mm if nch == 0 else None,
                              g_done[mp] if nch == 0 else None,
                              x_done[mp] if nch == 0 else None,
                              out_dma.get(mp - 1) if nch == 0 else None))
                    vv = P.do("vector", lambda v, sl=sl: v.tensor_tensor(
                        ob[:, sl], tm6[:], xtb[:, sl], OP.add))
                vv_done[mp] = vv
                out_dma[mp] = P.do("scalar", lambda sc, mp=mp: sc.dma_start(
                    d_out[mp * 128:(mp + 1) * 128, :], ob[:]), wait=(vv,), dma=True)
        P.do("scalar", lambda sc: sc.nop(), wait=(out_dma[31],))
        P.join()
    P.drain_dmas()


def _prep_shared(Wq, Wk_down, Wv_down, Wk_up, Wv_up, Wo,
                 qweight_gu, qzeros_gu, scales_gu, qweight_dn, qzeros_dn,
                 scales_dn):
    c = np.ascontiguousarray
    wq_p = c(Wq.T.reshape(32, 128, 32, 128).transpose(2, 1, 0, 3).reshape(H, 128, 4096))
    wo_p = c(Wo.T.reshape(32, 128, 32, 128).transpose(2, 1, 0, 3).reshape(32, 128, 4096))
    wkvd = np.empty((512, D), np.float32)
    for hh in range(8):
        wkvd[64 * hh: 64 * hh + 32] = Wk_down[32 * hh: 32 * hh + 32]
        wkvd[64 * hh + 32: 64 * hh + 64] = Wv_down[32 * hh: 32 * hh + 32]
    wkvd = c(wkvd.T)
    wkupT = Wk_up.T
    wvupT = Wv_up.T
    wkup4 = np.empty((4, 128, 1024), np.float32)
    wvup4 = np.empty((4, 128, 1024), np.float32)
    for j0 in range(4):
        wkup4[j0, 0:64] = wkup4[j0, 64:128] = wkupT[64 * j0: 64 * j0 + 64]
        wvup4[j0, 0:64] = wvup4[j0, 64:128] = wvupT[64 * j0: 64 * j0 + 64]
    qwgu_p = c(qweight_gu.reshape(32, 128, GU_M, 16).transpose(2, 1, 0, 3)
               .reshape(GU_M, 128, 512))
    qwdn_p = c(qweight_dn.reshape(86, 128, 32, 16).transpose(2, 1, 0, 3)
               .reshape(32, 128, 1376))
    indgu = np.zeros((128, 32 * 32), np.float32)
    for j in range(32):
        indgu[:, j * 32 + j] = -1.0
    inddn = np.zeros((128, 86 * 86), np.float32)
    for g in range(86):
        inddn[:, g * 86 + g] = -1.0
    mtri = np.triu(np.ones((128, 128), np.float32))
    return {
        "wq_p": wq_p, "wo_p": wo_p, "wkvd": wkvd, "wkup4": wkup4, "wvup4": wvup4,
        "qwgu_p": qwgu_p, "qzgu": c(qzeros_gu), "sgu": c(scales_gu),
        "qwdn_p": qwdn_p, "qzdn": c(qzeros_dn), "sdn": c(scales_dn),
        "indgu": indgu, "inddn": inddn, "mtri": mtri,
    }


def kernel(L0_attn_hidden_states, L0_attn_Wq, L0_attn_Wk_down, L0_attn_Wv_down,
           L0_attn_Wk_up, L0_attn_Wv_up, L0_attn_Wo, L0_ffn_qweight_gu,
           L0_ffn_qzeros_gu, L0_ffn_scales_gu, L0_ffn_qweight_dn,
           L0_ffn_qzeros_dn, L0_ffn_scales_dn):
    shared = _prep_shared(
        np.asarray(L0_attn_Wq), np.asarray(L0_attn_Wk_down),
        np.asarray(L0_attn_Wv_down), np.asarray(L0_attn_Wk_up),
        np.asarray(L0_attn_Wv_up), np.asarray(L0_attn_Wo),
        np.asarray(L0_ffn_qweight_gu), np.asarray(L0_ffn_qzeros_gu),
        np.asarray(L0_ffn_scales_gu), np.asarray(L0_ffn_qweight_dn),
        np.asarray(L0_ffn_qzeros_dn), np.asarray(L0_ffn_scales_dn))
    x = np.asarray(L0_attn_hidden_states)
    nc = build_program()
    in_maps = []
    for b in range(NCORE):
        m = dict(shared)
        m["xT"] = np.ascontiguousarray(x[b].T)
        in_maps.append(m)
    res = run_bass_kernel_spmd(nc, in_maps, core_ids=list(range(NCORE)))
    out = np.stack([np.asarray(res.results[b]["outT"]).T for b in range(NCORE)])
    return out.astype(np.float32)
